# version 06
# baseline (speedup 1.0000x reference)
import contextlib
import numpy as np
import concourse.bass as bass
import concourse.mybir as mybir
from concourse.bass_utils import run_bass_kernel_spmd

F32 = mybir.dt.float32
BF16 = mybir.dt.bfloat16
AF = mybir.ActivationFunctionType
ALU = mybir.AluOpType
AX = mybir.AxisListType

N_CORES = 8
T = 4096
H = 1024
NE = 8
NPASS = 4
PT = 1024
SEQ = 2048

_DEBUG_XS = False


class Walk:
    ENGS = ("t", "v", "a", "g", "s")

    def __init__(self, nc, tag):
        self.nc = nc
        self.tag = tag
        self.ops = {e: [] for e in self.ENGS}
        self.cnt = {}
        self.sems = {}
        self.waited = {e: {} for e in self.ENGS}
        self.last_rd = {}
        for k in ("mm", "vx", "ac", "sd", "gd"):
            name = tag + k
            self.sems[name] = nc.semaphore(name).__enter__()
            self.cnt[name] = 0
        self.MM, self.VX, self.AC, self.SD, self.GD = (
            tag + "mm", tag + "vx", tag + "ac", tag + "sd", tag + "gd")

    def wait(self, eng, name, val):
        if val is None or val <= 0:
            return
        assert val <= self.cnt[name], f"wait on future {name}:{val} (cnt={self.cnt[name]})"
        if self.waited[eng].get(name, 0) >= val:
            return
        self.waited[eng][name] = val
        s = self.sems[name]
        self.ops[eng].append(lambda e, s=s, v=val: e.wait_ge(s, v))

    def op(self, eng, fn, inc=None, by=1):
        if inc is None:
            self.ops[eng].append(fn)
            return None
        s = self.sems[inc]
        self.cnt[inc] += by
        v = self.cnt[inc]
        self.ops[eng].append(lambda e, fn=fn, s=s, by=by: fn(e).then_inc(s, by))
        return v

    def rd(self, res, kind, val):
        d = self.last_rd.setdefault(res, {})
        if val is not None and val > d.get(kind, 0):
            d[kind] = val

    def guard(self, eng, res):
        for kind, val in self.last_rd.get(res, {}).items():
            self.wait(eng, kind, val)

    def emit(self):
        with self.nc.Block() as blk:
            @blk.tensor
            def _(t):
                for f in self.ops["t"]:
                    f(t)

            @blk.vector
            def _(v):
                for f in self.ops["v"]:
                    f(v)

            @blk.scalar
            def _(a):
                for f in self.ops["a"]:
                    f(a)

            @blk.gpsimd
            def _(g):
                for f in self.ops["g"]:
                    f(g)

            @blk.sync
            def _(s):
                for f in self.ops["s"]:
                    f(s)


def build_program():
    nc = bass.Bass("TRN2", target_bir_lowering=False)

    x_in = nc.dram_tensor("x_in", [T, H], F32, kind="ExternalInput").ap()
    wq_d = nc.dram_tensor("wq", [H, H], F32, kind="ExternalInput").ap()
    wkv_d = nc.dram_tensor("wkv", [H, 2 * H], F32, kind="ExternalInput").ap()
    wo_d = nc.dram_tensor("wo", [H, H], F32, kind="ExternalInput").ap()
    bo_d = nc.dram_tensor("bo2d", [8, 128], F32, kind="ExternalInput").ap()
    gw_d = nc.dram_tensor("gw", [H, NE], F32, kind="ExternalInput").ap()
    wg_d = nc.dram_tensor("wg", [NE, H, 2048], F32, kind="ExternalInput").ap()
    wu_d = nc.dram_tensor("wu", [NE, H, 2048], F32, kind="ExternalInput").ap()
    wd_d = nc.dram_tensor("wd", [NE, 2048, H], F32, kind="ExternalInput").ap()
    wsg_d = nc.dram_tensor("wsg", [H, 2048], F32, kind="ExternalInput").ap()
    wsu_d = nc.dram_tensor("wsu", [H, 2048], F32, kind="ExternalInput").ap()
    wsd_d = nc.dram_tensor("wsd", [2048, H], F32, kind="ExternalInput").ap()
    y_out = nc.dram_tensor("y_out", [T, H], F32, kind="ExternalOutput").ap()
    kind_sc = "ExternalOutput" if _DEBUG_XS else "Internal"
    xs1 = nc.dram_tensor("xs1", [8, 128, T], F32, kind=kind_sc).ap()
    xs2 = nc.dram_tensor("xs2", [8, 128, T], F32, kind=kind_sc).ap()

    ident_d = nc.inline_tensor(np.eye(128, dtype=np.float32), name="ident").ap()
    sel_np = np.zeros((8, 8 * 128), dtype=np.float32)
    for e in range(8):
        sel_np[e, e * 128:(e + 1) * 128] = 1.0
    sel_d = nc.inline_tensor(sel_np, name="sel").ap()
    ones_d = nc.inline_tensor(np.ones((128, 128), dtype=np.float32), name="ones").ap()

    ctx = contextlib.ExitStack()
    ec = ctx.enter_context
    with ctx:
        ident_sb = ec(nc.sbuf_tensor("ident_sb", [128, 128], F32))
        ones_f = ec(nc.sbuf_tensor("ones_f", [128, 128], F32))
        ones_b = ec(nc.sbuf_tensor("ones_b", [128, 128], BF16))
        sel_sb = ec(nc.sbuf_tensor("sel_sb", [8, 1024], F32))
        gw_f = ec(nc.sbuf_tensor("gw_f", [128, 8 * 8], F32))
        sumg_bc = ec(nc.sbuf_tensor("sumg_bc", [128, 8], F32))
        bo_sb = ec(nc.sbuf_tensor("bo_sb", [128, 8], F32))
        half_c = ec(nc.sbuf_tensor("half_c", [128, 1024], F32))
        cm12 = ec(nc.sbuf_tensor("cm12", [128, 1], F32))
        ceps = ec(nc.sbuf_tensor("ceps", [128, 1], F32))

        with (nc.semaphore("su_d") as su_d, nc.semaphore("su_c") as su_c,
              nc.psum_tensor("su_ps", [128, 512], F32) as su_ps,
              nc.sbuf_tensor("su_tmp", [8, 128], F32) as su_tmp,
              nc.sbuf_tensor("su_col", [8, 1], F32) as su_col,
              nc.sbuf_tensor("su_row", [1, 8], F32) as su_row,
              nc.Block() as blk):

            @blk.gpsimd
            def _(g):
                g.memset(half_c[:, :], 0.5)
                g.memset(cm12[:, :], -12.0)
                g.memset(ceps[:, :], 1e-5)
                g.dma_start(ones_b[:, :], ones_d).then_inc(su_d, 16)
                g.dma_start(gw_f.ap().rearrange("p (kk e) -> p kk e", kk=8),
                            gw_d.rearrange("(kk p) e -> p kk e", p=128)).then_inc(su_d, 16)

            @blk.sync
            def _(s):
                s.dma_start(ident_sb[:, :], ident_d).then_inc(su_d, 16)
                s.dma_start(ones_f[:, :], ones_d).then_inc(su_d, 16)
                s.dma_start(sel_sb[:, :], sel_d).then_inc(su_d, 16)
                s.dma_start(su_tmp[:, :], bo_d).then_inc(su_d, 16)

            @blk.tensor
            def _(t):
                t.wait_ge(su_d, 16 * 6)
                t.transpose(su_ps[:, 0:8], su_tmp[:, :], ident_sb[0:8, 0:8])
                for k in range(7):
                    t.matmul(su_ps[0:8, 16:17], gw_f[:, k * 8:(k + 1) * 8], ones_f[:, 0:1],
                             start=(k == 0), stop=False)
                t.matmul(su_ps[0:8, 16:17], gw_f[:, 56:64], ones_f[:, 0:1],
                         start=False, stop=True).then_inc(su_c, 1)

            @blk.vector
            def _(v):
                v.wait_ge(su_c, 1)
                v.tensor_copy(bo_sb[:, :], su_ps[:, 0:8]).then_inc(su_c, 1)
                v.tensor_copy(su_col[:, :], su_ps[0:8, 16:17]).then_inc(su_c, 1)

            @blk.tensor
            def _(t):
                t.wait_ge(su_c, 3)
                t.transpose(su_ps[0:1, 32:40], su_col[:, :], ident_sb[0:8, 0:8]).then_inc(su_c, 1)

            @blk.scalar
            def _(sc):
                sc.wait_ge(su_c, 4)
                sc.copy(su_row[:, :], su_ps[0:1, 32:40]).then_inc(su_c, 1)

            @blk.tensor
            def _(t):
                t.wait_ge(su_c, 5)
                t.matmul(su_ps[:, 48:56], ones_f[0:1, :], su_row[:, :],
                         start=True, stop=True).then_inc(su_c, 1)

            @blk.vector
            def _(v):
                v.wait_ge(su_c, 6)
                v.tensor_copy(sumg_bc[:, :], su_ps[:, 48:56]).then_inc(su_c, 1)

        def ffn_phase(tag, src, dst, src_token_major, dst_token_major):
            pctx = contextlib.ExitStack()
            pec = pctx.enter_context
            with pctx:
                accs = [pec(nc.sbuf_tensor(f"{tag}_acc{i}", [128, 8 * PT], F32)) for i in range(2)]
                xln = pec(nc.sbuf_tensor(f"{tag}_xln", [128, 8 * PT], BF16))
                hid = pec(nc.sbuf_tensor(f"{tag}_hid", [128, 16 * PT], BF16))
                wgb = [pec(nc.sbuf_tensor(f"{tag}_wg{i}", [128, 8 * 256], BF16)) for i in range(2)]
                wub = [pec(nc.sbuf_tensor(f"{tag}_wu{i}", [128, 8 * 256], BF16)) for i in range(2)]
                wdb = [pec(nc.sbuf_tensor(f"{tag}_wd{i}", [128, 8 * 256], BF16)) for i in range(4)]
                s_sb = [pec(nc.sbuf_tensor(f"{tag}_s{i}", [128, 512], F32)) for i in range(2)]
                tmp_sb = [pec(nc.sbuf_tensor(f"{tag}_tmp{i}", [128, 512], F32)) for i in range(2)]
                w_bc = pec(nc.sbuf_tensor(f"{tag}_wbc", [128, 1024], F32))
                w_all = pec(nc.sbuf_tensor(f"{tag}_wall", [8, 1024], F32))
                mean_sb = pec(nc.sbuf_tensor(f"{tag}_mean", [128, 1024], F32))
                rstd_sb = pec(nc.sbuf_tensor(f"{tag}_rstd", [128, 1024], F32))
                var_sb = pec(nc.sbuf_tensor(f"{tag}_var", [128, 1024], F32))
                xb_bf = [pec(nc.sbuf_tensor(f"{tag}_xb{i}", [128, 512], BF16)) for i in range(4)]
                x2_bf = [pec(nc.sbuf_tensor(f"{tag}_x2{i}", [128, 512], BF16)) for i in range(4)]
                stage = [pec(nc.sbuf_tensor(f"{tag}_st{i}", [128, 1024], F32)) for i in range(2)]
                r1_sb = pec(nc.sbuf_tensor(f"{tag}_r1", [8, 1024], F32))
                mtok = pec(nc.sbuf_tensor(f"{tag}_mtok", [128, 1], F32))
                rtok = pec(nc.sbuf_tensor(f"{tag}_rtok", [128, 1], F32))
                rsm = [pec(nc.sbuf_tensor(f"{tag}_rs{i}", [128, 8], F32)) for i in range(6)]
                rsc = [pec(nc.sbuf_tensor(f"{tag}_rc{i}", [128, 1], F32)) for i in range(4)]
                wtok = pec(nc.sbuf_tensor(f"{tag}_wtok", [128, 8], F32))
                pg = [pec(nc.psum_tensor(f"{tag}_pg{i}", [128, 512], F32)) for i in range(2)]
                pu = [pec(nc.psum_tensor(f"{tag}_pu{i}", [128, 512], F32)) for i in range(2)]
                pd = [pec(nc.psum_tensor(f"{tag}_pd{i}", [128, 512], F32)) for i in range(2)]
                paux = pec(nc.psum_tensor(f"{tag}_paux", [128, 1024], F32))

                w = Walk(nc, tag)
                MM, VX, AC, SD, GD = w.MM, w.VX, w.AC, w.SD, w.GD
                ev = {}

                weight_mats = [(wg_d[e], wu_d[e], wd_d[e]) for e in range(NE)]
                weight_mats.append((wsg_d, wsu_d, wsd_d))

                NQ_TOT = NPASS * (NE + 1) * 8
                NH_TOT = NPASS * (NE + 1) * 4
                ev_qdma = {}
                ev_hdma = {}

                def emit_q(j):
                    if j >= NQ_TOT or j in ev_qdma:
                        return
                    e_ = (j // 8) % (NE + 1)
                    o_ = j % 8
                    bidx = j % 2
                    wg_s, wu_s, _ = weight_mats[e_]
                    w.guard("g", f"wgu{bidx}")
                    for (srcm, dstb) in ((wg_s, wgb[bidx]), (wu_s, wub[bidx])):
                        vd = w.op("g", lambda g, srcm=srcm, dstb=dstb, o_=o_: g.dma_start(
                            dstb.ap().rearrange("x (kk n) -> x kk n", kk=8),
                            srcm.rearrange("(kk x) d -> x kk d", x=128)[:, :, o_ * 256:(o_ + 1) * 256]),
                            inc=GD, by=16)
                    ev_qdma[j] = vd

                def emit_hq(j):
                    if j >= NH_TOT or j in ev_hdma:
                        return
                    e_ = (j // 4) % (NE + 1)
                    hq4 = j % 4
                    bpair = (j % 2) * 2
                    wd_s = weight_mats[e_][2]
                    w.guard("g", f"wd{bpair}")
                    for k2h in range(2):
                        vd = w.op("g", lambda g, wd_s=wd_s, hq4=hq4, k2h=k2h, bpair=bpair: g.dma_start(
                            wdb[bpair + k2h].ap().rearrange("x (kk n) -> x kk n", kk=8),
                            wd_s.rearrange("(kk x) n -> x kk n", x=128)[
                                :, 8 * k2h:8 * (k2h + 1), hq4 * 256:(hq4 + 1) * 256]),
                            inc=GD, by=16)
                    ev_hdma[j] = vd

                for j in range(2):
                    emit_q(j)
                    emit_hq(j)

                for p in range(NPASS):
                    accP = accs[p % 2]
                    accR = f"acc{p % 2}"
                    if src_token_major:
                        for r in range(8):
                            st_buf = stage[r % 2]
                            w.guard("s", f"stage{r % 2}")
                            rows = p * PT + r * 128
                            vd = w.op("s", lambda s, st_buf=st_buf, rows=rows: s.dma_start(
                                st_buf[:, :], src[rows:rows + 128, :]), inc=SD, by=16)
                            w.wait("t", SD, vd)
                            w.guard("t", "paux")
                            for f in range(8):
                                def trf(t, st_buf=st_buf, f=f):
                                    return t.transpose(paux[:, f * 128:(f + 1) * 128],
                                                       st_buf[:, f * 128:(f + 1) * 128], ident_sb[:, :])
                                if f < 7:
                                    w.op("t", trf)
                                else:
                                    etr = w.op("t", trf, inc=MM)
                            w.rd(f"stage{r % 2}", MM, etr)
                            w.wait("v", MM, etr)
                            w.guard("v", accR)
                            acc_v = accP.ap().rearrange("q (kk n) -> q kk n", kk=8)[:, :, r * 128:(r + 1) * 128]
                            pa_v = paux.ap().rearrange("q (kk n) -> q kk n", kk=8)
                            eev = w.op("v", lambda v, acc_v=acc_v, pa_v=pa_v: v.tensor_copy(acc_v, pa_v),
                                       inc=VX)
                            w.rd("paux", VX, eev)
                        ing_done = ("v", VX, eev)
                    else:
                        for k in range(8):
                            w.guard("s", accR)
                            vd = w.op("s", lambda s, k=k, p=p, accP=accP: s.dma_start(
                                accP[:, k * PT:(k + 1) * PT], src[k, :, p * PT:(p + 1) * PT]),
                                inc=SD, by=16)
                        ing_done = ("s", SD, vd)

                    ie0, isem0, ival0 = ing_done
                    for c in range(2):
                        w.wait("t", isem0, ival0)
                        w.guard("t", f"pdbank{c}")
                        for k in range(8):
                            def r1mm(t, k=k, c=c, accP=accP):
                                return t.matmul(pd[c][0:8, 0:512],
                                               gw_f[:, k * 8:(k + 1) * 8],
                                               accP[:, k * PT + c * 512: k * PT + c * 512 + 512],
                                               start=(k == 0), stop=(k == 7))
                            er1 = w.op("t", r1mm, inc=(MM if k == 7 else None))
                        w.rd(accR, MM, er1)
                        w.wait("a", MM, er1)
                        w.guard("a", "r1_sb")
                        er1e = w.op("a", lambda a, c=c: a.copy(
                            r1_sb[:, c * 512:(c + 1) * 512], pd[c][0:8, 0:512]), inc=AC)
                        ev[("r1ev", p, c)] = er1e
                        w.rd(f"pdbank{c}", AC, er1e)

                    ie, isem, ival = ing_done
                    for c in range(2):
                        for k in range(8):
                            i = c * 8 + k
                            bi = i % 4
                            if ie != "v":
                                w.wait("v", isem, ival)
                            w.wait("a", isem, ival)
                            w.guard("v", f"xb{bi}")
                            w.guard("a", f"x2{bi}")
                            asl = accP[:, k * PT + c * 512: k * PT + c * 512 + 512]
                            evb = w.op("v", lambda v, b=xb_bf[bi], asl=asl: v.tensor_copy(b[:, :], asl),
                                       inc=VX)
                            evs = w.op("a", lambda a, b=x2_bf[bi], asl=asl: a.activation(
                                b[:, :], asl, AF.Square), inc=AC)
                            w.rd(accR, VX, evb)
                            w.rd(accR, AC, evs)
                            w.wait("t", VX, evb)
                            w.wait("t", AC, evs)
                            if k == 0:
                                w.guard("t", "paux")
                            w.op("t", lambda t, b=xb_bf[bi], k=k: t.matmul(
                                paux[:, 0:512], ones_b[:, :], b[:, :], start=(k == 0), stop=(k == 7)))
                            def sqm(t, b=x2_bf[bi], k=k):
                                return t.matmul(paux[:, 512:1024], ones_b[:, :], b[:, :],
                                                start=(k == 0), stop=(k == 7))
                            smm = w.op("t", sqm, inc=MM)
                            w.rd(f"xb{bi}", MM, smm)
                            w.rd(f"x2{bi}", MM, smm)
                        w.wait("a", MM, smm)
                        w.guard("a", f"mean{c}")
                        ma = w.op("a", lambda a, c=c: a.activation(
                            mean_sb[:, c * 512:(c + 1) * 512], paux[:, 0:512], AF.Copy,
                            scale=1.0 / H), inc=AC)
                        mb = w.op("a", lambda a, c=c: a.activation(
                            var_sb[:, c * 512:(c + 1) * 512], paux[:, 512:1024], AF.Copy,
                            scale=1.0 / H), inc=AC)
                        w.rd("paux", AC, mb)
                        w.wait("v", AC, mb)
                        w.guard("v", f"rstd{c}")
                        e1 = w.op("v", lambda v, c=c: v.tensor_mul(
                            rstd_sb[:, c * 512:(c + 1) * 512], mean_sb[:, c * 512:(c + 1) * 512],
                            mean_sb[:, c * 512:(c + 1) * 512]), inc=VX)
                        w.wait("v", VX, e1)
                        e2 = w.op("v", lambda v, c=c: v.tensor_sub(
                            var_sb[:, c * 512:(c + 1) * 512], var_sb[:, c * 512:(c + 1) * 512],
                            rstd_sb[:, c * 512:(c + 1) * 512]), inc=VX)
                        w.wait("a", VX, e2)
                        e3 = w.op("a", lambda a, c=c: a.activation(
                            rstd_sb[:, c * 512:(c + 1) * 512], var_sb[:, c * 512:(c + 1) * 512],
                            AF.Sqrt, bias=ceps[:, 0:1]), inc=AC)
                        w.wait("v", AC, e3)
                        ev[("rstd", p, c)] = w.op("v", lambda v, c=c: v.reciprocal(
                            rstd_sb[:, c * 512:(c + 1) * 512], rstd_sb[:, c * 512:(c + 1) * 512]),
                            inc=VX)

                    for k in range(8):
                        for c in range(2):
                            w.wait("v", VX, ev[("rstd", p, c)])
                            w.guard("v", "xln")
                            w.guard("v", f"tmp{(k * 2 + c) % 2}")
                            asl = accP[:, k * PT + c * 512: k * PT + c * 512 + 512]
                            msl = mean_sb[:, c * 512:(c + 1) * 512]
                            rsl = rstd_sb[:, c * 512:(c + 1) * 512]
                            tb = tmp_sb[(k * 2 + c) % 2]
                            e1 = w.op("v", lambda v, tb=tb, asl=asl, msl=msl: v.tensor_sub(
                                tb[:, :], asl, msl), inc=VX)
                            w.wait("v", VX, e1)
                            w.rd(accR, VX, e1)
                            xsl = xln[:, k * PT + c * 512: k * PT + c * 512 + 512]
                            e2 = w.op("v", lambda v, xsl=xsl, tb=tb, rsl=rsl: v.tensor_mul(
                                xsl, tb[:, :], rsl), inc=VX)
                            w.rd(f"mean{c}", VX, e1)
                            w.rd(f"rstd{c}", VX, e2)
                            w.rd(f"tmp{(k * 2 + c) % 2}", VX, e2)
                    norm_done = e2

                    for bi_ in range(8):
                        c = bi_ // 4
                        w.wait("t", AC, ev[("r1ev", p, c)])
                        w.guard("t", "paux")
                        w.op("t", lambda t, bi_=bi_: t.transpose(
                            paux[:, 0:8], r1_sb[:, bi_ * 128:(bi_ + 1) * 128], ident_sb[0:8, 0:8]))
                        w.wait("t", VX, ev[("rstd", p, c)])
                        w.op("t", lambda t, bi_=bi_: t.transpose(
                            paux[:, 8:9], mean_sb[0:1, bi_ * 128:(bi_ + 1) * 128], ident_sb[0:1, 0:1]))
                        et3 = w.op("t", lambda t, bi_=bi_: t.transpose(
                            paux[:, 9:10], rstd_sb[0:1, bi_ * 128:(bi_ + 1) * 128], ident_sb[0:1, 0:1]),
                            inc=MM)
                        w.rd("r1_sb", MM, et3)
                        w.rd(f"mean{c}", MM, et3)
                        w.rd(f"rstd{c}", MM, et3)
                        w.wait("v", MM, et3)
                        e0 = w.op("v", lambda v: v.tensor_copy(mtok[:, :], paux[:, 8:9]), inc=VX)
                        e0b = w.op("v", lambda v: v.tensor_copy(rtok[:, :], paux[:, 9:10]), inc=VX)
                        e1 = w.op("v", lambda v: v.tensor_single_scalar(
                            rsm[0][:, :], sumg_bc[:, :], mtok[:, 0:1], ALU.mult), inc=VX)
                        w.wait("v", VX, e1)
                        e2 = w.op("v", lambda v: v.tensor_sub(rsm[1][:, :], paux[:, 0:8], rsm[0][:, :]),
                                  inc=VX)
                        w.rd("paux", VX, e2)
                        w.wait("v", VX, e2)
                        e3 = w.op("v", lambda v: v.tensor_single_scalar(
                            rsm[1][:, :], rsm[1][:, :], rtok[:, 0:1], ALU.mult), inc=VX)
                        w.wait("v", VX, e3)
                        e4 = w.op("v", lambda v: v.reduce_max(rsc[0][:, :], rsm[1][:, :], AX.X,
                                                              negate=True), inc=VX)
                        w.wait("a", VX, e4)
                        e5 = w.op("a", lambda a: a.activation(
                            rsm[2][:, :], rsm[1][:, :], AF.Exp, bias=rsc[0][:, 0:1]), inc=AC)
                        w.wait("v", AC, e5)
                        e6 = w.op("v", lambda v: v.reduce_sum(rsc[1][:, :], rsm[2][:, :], AX.X), inc=VX)
                        w.wait("v", VX, e6)
                        e7 = w.op("v", lambda v: v.reciprocal(rsc[2][:, :], rsc[1][:, :]), inc=VX)
                        w.wait("v", VX, e7)
                        e8 = w.op("v", lambda v: v.tensor_single_scalar(
                            rsm[2][:, :], rsm[2][:, :], rsc[2][:, 0:1], ALU.mult), inc=VX)
                        w.wait("v", VX, e8)
                        e9 = w.op("v", lambda v: v.reduce_max(rsc[0][:, :], rsm[2][:, :], AX.X,
                                                              negate=True), inc=VX)
                        w.wait("v", VX, e9)
                        e10 = w.op("v", lambda v: v.tensor_scalar_mul(rsc[3][:, :], rsc[0][:, :], -1.0),
                                   inc=VX)
                        w.wait("v", VX, e10)
                        e11 = w.op("v", lambda v: v.tensor_single_scalar(
                            rsm[3][:, :], rsm[2][:, :], rsc[3][:, 0:1], ALU.is_ge), inc=VX)
                        w.wait("v", VX, e11)
                        e12 = w.op("v", lambda v: v.scalar_tensor_tensor(
                            rsm[4][:, :], rsm[3][:, :], -1e30, rsm[2][:, :], ALU.mult, ALU.add), inc=VX)
                        w.wait("v", VX, e12)
                        e13 = w.op("v", lambda v: v.reduce_max(rsc[0][:, :], rsm[4][:, :], AX.X), inc=VX)
                        w.wait("v", VX, e13)
                        e14 = w.op("v", lambda v: v.tensor_single_scalar(
                            rsm[5][:, :], rsm[2][:, :], rsc[0][:, 0:1], ALU.is_ge), inc=VX)
                        w.wait("v", VX, e14)
                        e15 = w.op("v", lambda v: v.tensor_mul(rsm[5][:, :], rsm[5][:, :], rsm[2][:, :]),
                                   inc=VX)
                        w.wait("v", VX, e15)
                        e16 = w.op("v", lambda v: v.reduce_sum(rsc[1][:, :], rsm[5][:, :], AX.X), inc=VX)
                        w.wait("v", VX, e16)
                        e17 = w.op("v", lambda v: v.tensor_scalar(
                            rsc[1][:, :], rsc[1][:, :], 2.0, 2e-20, ALU.mult, ALU.add), inc=VX)
                        w.wait("v", VX, e17)
                        e18 = w.op("v", lambda v: v.reciprocal(rsc[2][:, :], rsc[1][:, :]), inc=VX)
                        w.wait("v", VX, e18)
                        w.guard("v", "wtok")
                        e19 = w.op("v", lambda v: v.tensor_single_scalar(
                            wtok[:, :], rsm[5][:, :], rsc[2][:, 0:1], ALU.mult), inc=VX)
                        w.wait("t", VX, e19)
                        ewt = w.op("t", lambda t: t.transpose(
                            paux[0:8, 16:144], wtok[:, :], ident_sb[:, :]), inc=MM)
                        w.rd("wtok", MM, ewt)
                        w.wait("a", MM, ewt)
                        w.guard("a", "w_all")
                        ewe = w.op("a", lambda a, bi_=bi_: a.copy(
                            w_all[:, bi_ * 128:(bi_ + 1) * 128], paux[0:8, 16:144]), inc=AC)
                        w.rd("paux", AC, ewe)
                        ev[("wall", p, bi_)] = ewe
                    router_done = ev[("wall", p, 7)]

                    for e in range(NE + 1):
                        shared = (e == NE)
                        pe_idx = p * (NE + 1) + e
                        if not shared:
                            w.wait("t", AC, router_done)
                            for c in range(2):
                                w.guard("t", "paux")
                                ewm = w.op("t", lambda t, e=e, c=c: t.matmul(
                                    paux[:, c * 512:(c + 1) * 512],
                                    sel_sb[:, e * 128:(e + 1) * 128],
                                    w_all[:, c * 512:(c + 1) * 512],
                                    start=True, stop=True), inc=MM)
                                w.rd("w_all", MM, ewm)
                                w.wait("a", MM, ewm)
                                w.guard("a", f"wbc{c}")
                                ewc = w.op("a", lambda a, c=c: a.copy(
                                    w_bc[:, c * 512:(c + 1) * 512], paux[:, c * 512:(c + 1) * 512]),
                                    inc=AC)
                                w.rd("paux", AC, ewc)
                                ev[("wbc", p, e, c)] = ewc
                            wbc_t = w_bc
                        else:
                            wbc_t = half_c

                        for d in range(16):
                            oct_ = d // 2
                            jq = pe_idx * 8 + oct_
                            bidx = jq % 2
                            do = d % 2
                            for c in range(2):
                                w.wait("t", GD, ev_qdma[jq])
                                w.wait("t", VX, norm_done)
                                w.guard("t", f"pgbank{c}")
                                for k in range(8):
                                    def gmm(t, bidx=bidx, do=do, k=k, c=c):
                                        return t.matmul(
                                            pg[c][:, :],
                                            wgb[bidx][:, k * 256 + do * 128: k * 256 + do * 128 + 128],
                                            xln[:, k * PT + c * 512: k * PT + c * 512 + 512],
                                            start=(k == 0), stop=(k == 7))
                                    egm = w.op("t", gmm, inc=(MM if k == 7 else None))
                                w.guard("t", f"pubank{c}")
                                for k in range(8):
                                    def umm(t, bidx=bidx, do=do, k=k, c=c):
                                        return t.matmul(
                                            pu[c][:, :],
                                            wub[bidx][:, k * 256 + do * 128: k * 256 + do * 128 + 128],
                                            xln[:, k * PT + c * 512: k * PT + c * 512 + 512],
                                            start=(k == 0), stop=(k == 7))
                                    eum = w.op("t", umm, inc=(MM if k == 7 else None))
                                w.rd("xln", MM, eum)
                                w.rd(f"wgu{bidx}", MM, eum)
                                w.wait("a", MM, egm)
                                w.guard("a", f"s{c}")
                                esl = w.op("a", lambda a, c=c: a.activation(
                                    s_sb[c][:, :], pg[c][:, :], AF.Silu), inc=AC)
                                w.rd(f"pgbank{c}", AC, esl)
                                w.wait("v", MM, eum)
                                w.wait("v", AC, esl)
                                w.guard("v", f"hid{c}")
                                hsl = hid[:, d * PT + c * 512: d * PT + c * 512 + 512]
                                eml = w.op("v", lambda v, hsl=hsl, c=c: v.tensor_mul(
                                    hsl, s_sb[c][:, :], pu[c][:, :]), inc=VX)
                                w.rd(f"pubank{c}", VX, eml)
                                w.rd(f"s{c}", VX, eml)
                                ev[("mult", p, e, d, c)] = eml
                            if d % 2 == 1:
                                emit_q(pe_idx * 8 + d // 2 + 2)

                        for h in range(8):
                            hq4 = h // 2
                            jh = pe_idx * 4 + hq4
                            bpair = (jh % 2) * 2
                            ho = h % 2
                            for c in range(2):
                                w.wait("t", GD, ev_hdma[jh])
                                w.wait("t", VX, ev[("mult", p, e, 15, c)])
                                w.guard("t", f"pdbank{c}")
                                for k2 in range(16):
                                    def dmm(t, bpair=bpair, k2=k2, ho=ho, c=c):
                                        return t.matmul(
                                            pd[c][:, :],
                                            wdb[bpair + k2 // 8][:, (k2 % 8) * 256 + ho * 128:
                                                                 (k2 % 8) * 256 + ho * 128 + 128],
                                            hid[:, k2 * PT + c * 512: k2 * PT + c * 512 + 512],
                                            start=(k2 == 0), stop=(k2 == 15))
                                    edm = w.op("t", dmm, inc=(MM if k2 == 15 else None))
                                w.rd(f"hid{c}", MM, edm)
                                w.rd(f"wd{bpair}", MM, edm)
                                w.wait("v", MM, edm)
                                if not shared:
                                    w.wait("v", AC, ev[("wbc", p, e, c)])
                                tb = tmp_sb[c]
                                w.guard("v", f"tmp{c}")
                                etm = w.op("v", lambda v, tb=tb, c=c, wbc_t=wbc_t: v.tensor_mul(
                                    tb[:, :], wbc_t[:, c * 512:(c + 1) * 512], pd[c][:, :]), inc=VX)
                                w.rd(f"pdbank{c}", VX, etm)
                                if not shared:
                                    w.rd(f"wbc{c}", VX, etm)
                                w.wait("v", VX, etm)
                                asl = accP[:, h * PT + c * 512: h * PT + c * 512 + 512]
                                ead = w.op("v", lambda v, asl=asl, tb=tb: v.tensor_add(
                                    asl, asl, tb[:, :]), inc=VX)
                                w.rd(f"tmp{c}", VX, ead)
                                ev[("accadd", p, e, h, c)] = ead
                            if h % 2 == 1:
                                emit_hq(pe_idx * 4 + h // 2 + 2)
                    if dst_token_major:
                        for r in range(8):
                            c = r // 4
                            ob = stage[r % 2]
                            w.wait("t", VX, ev[("accadd", p, NE, 7, c)])
                            w.guard("t", "paux")
                            for f in range(8):
                                def etr2(t, f=f, r=r, accP=accP):
                                    return t.transpose(paux[:, f * 128:(f + 1) * 128],
                                                       accP[:, f * PT + r * 128: f * PT + r * 128 + 128],
                                                       ident_sb[:, :])
                                etr = w.op("t", etr2, inc=(MM if f == 7 else None))
                            w.rd(accR, MM, etr)
                            w.wait("v", MM, etr)
                            w.guard("v", f"stage{r % 2}")
                            eev = w.op("v", lambda v, ob=ob: v.tensor_copy(ob[:, :], paux[:, :]), inc=VX)
                            w.rd("paux", VX, eev)
                            w.wait("s", VX, eev)
                            rows = p * PT + r * 128
                            edr = w.op("s", lambda s, ob=ob, rows=rows: s.dma_start(
                                dst[rows:rows + 128, :], ob[:, :]), inc=SD, by=16)
                            w.rd(f"stage{r % 2}", SD, edr)
                    else:
                        for k in range(8):
                            w.wait("s", VX, ev[("accadd", p, NE, k, 1)])
                            edr = w.op("s", lambda s, k=k, p=p, accP=accP: s.dma_start(
                                dst[k, :, p * PT:(p + 1) * PT], accP[:, k * PT:(k + 1) * PT]),
                                inc=SD, by=16)
                            w.rd(accR, SD, edr)

                w.wait("s", SD, w.cnt[SD])
                w.wait("g", GD, w.cnt[GD])
                w.emit()

        def attn_phase(tag, src, dst):
            pctx = contextlib.ExitStack()
            pec = pctx.enter_context
            with pctx:
                xln2 = pec(nc.sbuf_tensor(f"{tag}_xln", [128, 8 * SEQ], BF16))
                attn = xln2
                q_bf = pec(nc.sbuf_tensor(f"{tag}_q", [128, 8 * SEQ], BF16))
                k_bf = pec(nc.sbuf_tensor(f"{tag}_k", [128, 8 * SEQ], BF16))
                vT = pec(nc.sbuf_tensor(f"{tag}_vT", [128, 16 * H], BF16))
                wbuf = [pec(nc.sbuf_tensor(f"{tag}_wb{i}", [128, 8 * 1024], BF16)) for i in range(2)]
                xc = [pec(nc.sbuf_tensor(f"{tag}_xc{i}", [128, 512], F32)) for i in range(4)]
                xbb = [pec(nc.sbuf_tensor(f"{tag}_xbb{i}", [128, 512], BF16)) for i in range(2)]
                x2b = [pec(nc.sbuf_tensor(f"{tag}_x2b{i}", [128, 512], BF16)) for i in range(2)]
                mean_sb = pec(nc.sbuf_tensor(f"{tag}_mean", [128, 1024], F32))
                rstd_sb = pec(nc.sbuf_tensor(f"{tag}_rstd", [128, 1024], F32))
                var_sb = pec(nc.sbuf_tensor(f"{tag}_var", [128, 1024], F32))
                probs = [pec(nc.sbuf_tensor(f"{tag}_pr{i}", [128, 1024], BF16)) for i in range(2)]
                inv_sb = [pec(nc.sbuf_tensor(f"{tag}_inv0", [128, 512], F32))] * 2
                t_sb = [pec(nc.sbuf_tensor(f"{tag}_t{i}", [128, 512], F32)) for i in range(2)]
                o_sb = [pec(nc.sbuf_tensor(f"{tag}_o{i}", [128, 512], F32)) for i in range(2)]
                psS = [pec(nc.psum_tensor(f"{tag}_pS{i}", [128, 1024], F32)) for i in range(2)]
                pav = pec(nc.psum_tensor(f"{tag}_pav", [128, 512], F32))
                psm = pec(nc.psum_tensor(f"{tag}_psm", [128, 512], F32))
                po = [pec(nc.psum_tensor(f"{tag}_po{i}", [128, 512], F32)) for i in range(2)]

                w = Walk(nc, tag)
                MM, VX, AC, SD, GD = w.MM, w.VX, w.AC, w.SD, w.GD
                ev = {}

                for b in range(2):
                    base = b * SEQ
                    for c in range(4):
                        cc = c % 2
                        for k in range(8):
                            i = c * 8 + k
                            xb = xc[i % 4]
                            w.guard("s", f"xc{i % 4}")
                            vd = w.op("s", lambda s, xb=xb, k=k, c=c, base=base: s.dma_start(
                                xb[:, :], src[k, :, base + c * 512: base + c * 512 + 512]),
                                inc=SD, by=16)
                            w.wait("v", SD, vd)
                            w.wait("a", SD, vd)
                            w.guard("v", f"xbb{i % 2}")
                            w.guard("a", f"x2b{i % 2}")
                            evb = w.op("v", lambda v, xb=xb, j=i % 2: v.tensor_copy(
                                xbb[j][:, :], xb[:, :]), inc=VX)
                            evs = w.op("a", lambda a, xb=xb, j=i % 2: a.activation(
                                x2b[j][:, :], xb[:, :], AF.Square), inc=AC)
                            w.rd(f"xc{i % 4}", VX, evb)
                            w.rd(f"xc{i % 4}", AC, evs)
                            w.wait("t", VX, evb)
                            w.wait("t", AC, evs)
                            if k == 0:
                                w.guard("t", "po0")
                                w.guard("t", "po1")
                            w.op("t", lambda t, j=i % 2, k=k: t.matmul(
                                po[0][:, :], ones_b[:, :], xbb[j][:, :], start=(k == 0), stop=(k == 7)))
                            smm = w.op("t", lambda t, j=i % 2, k=k: t.matmul(
                                po[1][:, :], ones_b[:, :], x2b[j][:, :], start=(k == 0), stop=(k == 7)),
                                inc=MM)
                            w.rd(f"xbb{i % 2}", MM, smm)
                            w.rd(f"x2b{i % 2}", MM, smm)
                        w.wait("a", MM, smm)
                        w.guard("a", f"mean{cc}")
                        ma = w.op("a", lambda a, cc=cc: a.activation(
                            mean_sb[:, cc * 512:(cc + 1) * 512], po[0][:, :], AF.Copy, scale=1.0 / H),
                            inc=AC)
                        mb = w.op("a", lambda a, cc=cc: a.activation(
                            var_sb[:, cc * 512:(cc + 1) * 512], po[1][:, :], AF.Copy, scale=1.0 / H),
                            inc=AC)
                        w.rd("po0", AC, mb)
                        w.rd("po1", AC, mb)
                        w.wait("v", AC, mb)
                        w.guard("v", f"rstd{cc}")
                        e1 = w.op("v", lambda v, cc=cc: v.tensor_mul(
                            rstd_sb[:, cc * 512:(cc + 1) * 512], mean_sb[:, cc * 512:(cc + 1) * 512],
                            mean_sb[:, cc * 512:(cc + 1) * 512]), inc=VX)
                        w.wait("v", VX, e1)
                        e2 = w.op("v", lambda v, cc=cc: v.tensor_sub(
                            var_sb[:, cc * 512:(cc + 1) * 512], var_sb[:, cc * 512:(cc + 1) * 512],
                            rstd_sb[:, cc * 512:(cc + 1) * 512]), inc=VX)
                        w.wait("a", VX, e2)
                        e3 = w.op("a", lambda a, cc=cc: a.activation(
                            rstd_sb[:, cc * 512:(cc + 1) * 512], var_sb[:, cc * 512:(cc + 1) * 512],
                            AF.Sqrt, bias=ceps[:, 0:1]), inc=AC)
                        w.wait("v", AC, e3)
                        erc = w.op("v", lambda v, cc=cc: v.reciprocal(
                            rstd_sb[:, cc * 512:(cc + 1) * 512], rstd_sb[:, cc * 512:(cc + 1) * 512]),
                            inc=VX)
                        for k in range(8):
                            i = c * 8 + k
                            xb = xc[i % 4]
                            w.guard("s", f"xc{i % 4}")
                            vd = w.op("s", lambda s, xb=xb, k=k, c=c, base=base: s.dma_start(
                                xb[:, :], src[k, :, base + c * 512: base + c * 512 + 512]),
                                inc=SD, by=16)
                            w.wait("v", SD, vd)
                            w.wait("v", VX, erc)
                            w.guard("v", f"t{k % 2}")
                            w.guard("v", "xln2")
                            tb = t_sb[k % 2]
                            e4 = w.op("v", lambda v, tb=tb, xb=xb, cc=cc: v.tensor_sub(
                                tb[:, :], xb[:, :], mean_sb[:, cc * 512:(cc + 1) * 512]), inc=VX)
                            w.rd(f"xc{i % 4}", VX, e4)
                            w.rd(f"mean{cc}", VX, e4)
                            w.wait("v", VX, e4)
                            xsl = xln2[:, k * SEQ + c * 512: k * SEQ + c * 512 + 512]
                            e5 = w.op("v", lambda v, xsl=xsl, tb=tb, cc=cc: v.tensor_mul(
                                xsl, tb[:, :], rstd_sb[:, cc * 512:(cc + 1) * 512]), inc=VX)
                            w.rd(f"t{k % 2}", VX, e5)
                            w.rd(f"rstd{cc}", VX, e5)
                    ln_done = e5

                    w.guard("g", "wbuf0")
                    ev_wq = w.op("g", lambda g: g.dma_start(
                        wbuf[0].ap().rearrange("x (kk n) -> x kk n", kk=8),
                        wq_d.rearrange("(kk x) n -> x kk n", x=128)), inc=GD, by=16)
                    w.guard("g", "wbuf1")
                    ev_wk = w.op("g", lambda g: g.dma_start(
                        wbuf[1].ap().rearrange("x (kk n) -> x kk n", kk=8),
                        wkv_d.rearrange("(kk x) n -> x kk n", x=128)[:, :, 0:1024]), inc=GD, by=16)

                    def proj(wb_i, out_sb, out_res, wdma_ev, evac_eng, res_tag):
                        last = None
                        for m in range(8):
                            for c in range(4):
                                i = m * 4 + c
                                pb = po[i % 2]
                                w.wait("t", GD, wdma_ev)
                                w.wait("t", VX, ln_done)
                                w.guard("t", f"po{i % 2}")
                                for k in range(8):
                                    def pmm(t, pb=pb, m=m, c=c, k=k, wb_i=wb_i):
                                        return t.matmul(
                                            pb[:, :],
                                            wbuf[wb_i][:, k * 1024 + m * 128: k * 1024 + m * 128 + 128],
                                            xln2[:, k * SEQ + c * 512: k * SEQ + c * 512 + 512],
                                            start=(k == 0), stop=(k == 7))
                                    emm = w.op("t", pmm, inc=(MM if k == 7 else None))
                                w.rd(f"wbuf{wb_i}", MM, emm)
                                w.rd("xln2", MM, emm)
                                w.wait(evac_eng, MM, emm)
                                osl = out_sb[:, m * SEQ + c * 512: m * SEQ + c * 512 + 512]
                                if evac_eng == "v":
                                    eev = w.op("v", lambda v, osl=osl, pb=pb: v.tensor_copy(osl, pb[:, :]),
                                               inc=VX)
                                    w.rd(f"po{i % 2}", VX, eev)
                                else:
                                    eev = w.op("a", lambda a, osl=osl, pb=pb: a.copy(osl, pb[:, :]),
                                               inc=AC)
                                    w.rd(f"po{i % 2}", AC, eev)
                                ev[(res_tag, b, m, c)] = eev
                                last = emm
                        return last

                    qproj_mm = proj(0, q_bf, None, ev_wq, "v", "qev")
                    kproj_mm = proj(1, k_bf, None, ev_wk, "v", "kev")

                    w.guard("g", "wbuf0")
                    ev_wv = w.op("g", lambda g: g.dma_start(
                        wbuf[0].ap().rearrange("x (kk n) -> x kk n", kk=8),
                        wkv_d.rearrange("(kk x) n -> x kk n", x=128)[:, :, 1024:2048]), inc=GD, by=16)

                    for kc in range(16):
                        for n in range(2):
                            i = kc * 2 + n
                            pb = po[i % 2]
                            w.wait("t", GD, ev_wv)
                            w.guard("t", f"po{i % 2}")
                            for k in range(8):
                                def vmm(t, pb=pb, kc=kc, n=n, k=k):
                                    return t.matmul(
                                        pb[:, :],
                                        xln2[:, k * SEQ + kc * 128: k * SEQ + kc * 128 + 128],
                                        wbuf[0][:, k * 1024 + n * 512: k * 1024 + n * 512 + 512],
                                        start=(k == 0), stop=(k == 7))
                                emm = w.op("t", vmm, inc=(MM if k == 7 else None))
                            w.rd("wbuf0", MM, emm)
                            w.rd("xln2", MM, emm)
                            w.wait("a", MM, emm)
                            w.guard("a", "vT")
                            vsl = vT[:, kc * H + n * 512: kc * H + n * 512 + 512]
                            eev = w.op("a", lambda a, vsl=vsl, pb=pb: a.copy(vsl, pb[:, :]), inc=AC)
                            w.rd(f"po{i % 2}", AC, eev)
                    vT_done = eev

                    w.guard("g", "wbuf1")
                    ev_wo = w.op("g", lambda g: g.dma_start(
                        wbuf[1].ap().rearrange("x (kk n) -> x kk n", kk=8),
                        wo_d.rearrange("(kk x) n -> x kk n", x=128)), inc=GD, by=16)

                    for hp in range(8):
                        hA = 2 * hp
                        for qg in range(4):
                            gidx = hp * 4 + qg
                            w.guard("t", "pav")
                            w.guard("t", "psm")

                            def emit_scores(kc, hp=hp, qg=qg):
                                j = kc % 2
                                sS = psS[j]
                                w.wait("t", VX, ev[("kev", b, hp, 3)])
                                w.wait("t", VX, ev[("qev", b, hp, 3)])
                                w.guard("t", f"psS{j}")
                                def smmA(t, sS=sS, kc=kc, hp=hp, qg=qg):
                                    return t.matmul(
                                        sS[:, 0:512],
                                        k_bf[0:64, hp * SEQ + kc * 128: hp * SEQ + kc * 128 + 128],
                                        q_bf[0:64, hp * SEQ + qg * 512: hp * SEQ + qg * 512 + 512],
                                        start=True, stop=True, tile_position=(0, 0))
                                w.op("t", smmA)
                                def smmB(t, sS=sS, kc=kc, hp=hp, qg=qg):
                                    return t.matmul(
                                        sS[:, 512:1024],
                                        k_bf[64:128, hp * SEQ + kc * 128: hp * SEQ + kc * 128 + 128],
                                        q_bf[64:128, hp * SEQ + qg * 512: hp * SEQ + qg * 512 + 512],
                                        start=True, stop=True, tile_position=(64, 0))
                                return w.op("t", smmB, inc=MM)

                            ev_s = {0: emit_scores(0)}
                            for kc in range(16):
                                j = kc % 2
                                if kc + 1 < 16:
                                    ev_s[kc + 1] = emit_scores(kc + 1)
                                pP = probs[j]
                                w.wait("a", MM, ev_s[kc])
                                w.guard("a", f"pr{j}")
                                eex = w.op("a", lambda a, pP=pP, sS=psS[j]: a.activation(
                                    pP[:, :], sS[:, :], AF.Exp, bias=cm12[:, 0:1], scale=0.125),
                                    inc=AC)
                                w.rd(f"psS{j}", AC, eex)
                                w.wait("t", AC, eex)
                                if kc == 0:
                                    w.wait("t", AC, vT_done)
                                def avA(t, pP=pP, hA=hA, kc=kc):
                                    return t.matmul(
                                        pav[0:64, :],
                                        vT[:, kc * H + hA * 64: kc * H + hA * 64 + 64],
                                        pP[:, 0:512], start=(kc == 0), stop=(kc == 15),
                                        tile_position=(0, 0))
                                def avB(t, pP=pP, hA=hA, kc=kc):
                                    return t.matmul(
                                        pav[64:128, :],
                                        vT[:, kc * H + hA * 64 + 64: kc * H + hA * 64 + 128],
                                        pP[:, 512:1024], start=(kc == 0), stop=(kc == 15),
                                        tile_position=(0, 64))
                                def smA(t, pP=pP, kc=kc):
                                    return t.matmul(
                                        psm[0:64, :], ones_b[:, 0:64], pP[:, 0:512],
                                        start=(kc == 0), stop=(kc == 15), tile_position=(0, 0))
                                def smB(t, pP=pP, kc=kc):
                                    return t.matmul(
                                        psm[64:128, :], ones_b[:, 0:64], pP[:, 512:1024],
                                        start=(kc == 0), stop=(kc == 15), tile_position=(0, 64))
                                w.op("t", avA)
                                w.op("t", avB)
                                w.op("t", smA)
                                eprd = w.op("t", smB, inc=MM)
                                w.rd(f"pr{j}", MM, eprd)
                                w.rd("vT", MM, eprd)
                            w.wait("v", MM, eprd)
                            iv = inv_sb[0]
                            w.guard("v", "inv0")
                            er = w.op("v", lambda v, iv=iv: v.reciprocal(iv[:, :], psm[:, :]), inc=VX)
                            w.rd("psm", VX, er)
                            w.wait("v", VX, er)
                            w.guard("v", "xln2")
                            asl = attn[:, hp * SEQ + qg * 512: hp * SEQ + qg * 512 + 512]
                            eat = w.op("v", lambda v, asl=asl, iv=iv: v.tensor_mul(
                                asl, pav[:, :], iv[:, :]), inc=VX)
                            w.rd("pav", VX, eat)
                            w.rd("inv0", VX, eat)
                            ev[("atev", b, hp, qg)] = eat

                    for m in range(8):
                        for c in range(4):
                            i = m * 4 + c
                            pb = po[i % 2]
                            w.wait("t", GD, ev_wo)
                            for hp in range(8):
                                w.wait("t", VX, ev[("atev", b, hp, c)])
                            w.guard("t", f"po{i % 2}")
                            for k in range(8):
                                def omm(t, pb=pb, m=m, c=c, k=k):
                                    return t.matmul(
                                        pb[:, :],
                                        wbuf[1][:, k * 1024 + m * 128: k * 1024 + m * 128 + 128],
                                        attn[:, k * SEQ + c * 512: k * SEQ + c * 512 + 512],
                                        start=(k == 0), stop=(k == 7))
                                emm = w.op("t", omm, inc=(MM if k == 7 else None))
                            w.rd("wbuf1", MM, emm)
                            w.rd("xln2", MM, emm)
                            ob = o_sb[i % 2]
                            xb = xc[i % 4]
                            w.guard("s", f"xc{i % 4}")
                            vd = w.op("s", lambda s, xb=xb, m=m, c=c, base=base: s.dma_start(
                                xb[:, :], src[m, :, base + c * 512: base + c * 512 + 512]),
                                inc=SD, by=16)
                            w.wait("v", SD, vd)
                            w.wait("v", MM, emm)
                            w.guard("v", f"o{i % 2}")
                            ead = w.op("v", lambda v, ob=ob, pb=pb, xb=xb, m=m: v.scalar_tensor_tensor(
                                ob[:, :], pb[:, :], bo_sb[:, m:m + 1], xb[:, :], ALU.add, ALU.add),
                                inc=VX)
                            w.rd(f"po{i % 2}", VX, ead)
                            w.rd(f"xc{i % 4}", VX, ead)
                            w.wait("s", VX, ead)
                            eds = w.op("s", lambda s, ob=ob, m=m, c=c, base=base: s.dma_start(
                                dst[m, :, base + c * 512: base + c * 512 + 512], ob[:, :]),
                                inc=SD, by=16)
                            w.rd(f"o{i % 2}", SD, eds)

                w.wait("s", SD, w.cnt[SD])
                w.wait("g", GD, w.cnt[GD])
                w.emit()

        ffn_phase("A", x_in, xs1, True, False)
        attn_phase("B", xs1, xs2)
        ffn_phase("C", xs2, y_out, False, True)

    return nc


_NC_CACHE = None


def kernel(L0_attn_hidden_states, L0_attn_Wq, L0_attn_Wkv, L0_attn_Wo, L0_attn_bo,
           L0_ffn_gate_w, L0_ffn_w_gate, L0_ffn_w_up, L0_ffn_w_down,
           L0_ffn_ws_gate, L0_ffn_ws_up, L0_ffn_ws_down):
    global _NC_CACHE
    x = np.ascontiguousarray(np.asarray(L0_attn_hidden_states, dtype=np.float32))
    common = {
        "wq": np.ascontiguousarray(np.asarray(L0_attn_Wq, dtype=np.float32)),
        "wkv": np.ascontiguousarray(np.asarray(L0_attn_Wkv, dtype=np.float32)),
        "wo": np.ascontiguousarray(np.asarray(L0_attn_Wo, dtype=np.float32)),
        "bo2d": np.ascontiguousarray(np.asarray(L0_attn_bo, dtype=np.float32).reshape(8, 128)),
        "gw": np.ascontiguousarray(np.asarray(L0_ffn_gate_w, dtype=np.float32)),
        "wg": np.ascontiguousarray(np.asarray(L0_ffn_w_gate, dtype=np.float32)),
        "wu": np.ascontiguousarray(np.asarray(L0_ffn_w_up, dtype=np.float32)),
        "wd": np.ascontiguousarray(np.asarray(L0_ffn_w_down, dtype=np.float32)),
        "wsg": np.ascontiguousarray(np.asarray(L0_ffn_ws_gate, dtype=np.float32)),
        "wsu": np.ascontiguousarray(np.asarray(L0_ffn_ws_up, dtype=np.float32)),
        "wsd": np.ascontiguousarray(np.asarray(L0_ffn_ws_down, dtype=np.float32)),
    }
    in_maps = []
    for i in range(N_CORES):
        m = dict(common)
        m["x_in"] = np.ascontiguousarray(x[2 * i:2 * i + 2].reshape(T, H))
        in_maps.append(m)

    if _NC_CACHE is None:
        _NC_CACHE = build_program()
    res = run_bass_kernel_spmd(_NC_CACHE, in_maps, core_ids=list(range(N_CORES)))
    out = np.concatenate([r["y_out"].reshape(2, SEQ, H) for r in res.results], axis=0)
    return out


# version 08
# speedup vs baseline: 1.0852x; 1.0852x over previous
import contextlib
import numpy as np
import concourse.bass as bass
import concourse.mybir as mybir
from concourse.bass_utils import run_bass_kernel_spmd

F32 = mybir.dt.float32
BF16 = mybir.dt.bfloat16
AF = mybir.ActivationFunctionType
ALU = mybir.AluOpType
AX = mybir.AxisListType

N_CORES = 8
T = 4096
H = 1024
NE = 8
NPASS = 4
PT = 1024
SEQ = 2048

_DEBUG_XS = False


class Walk:
    ENGS = ("t", "v", "a", "g", "s")

    def __init__(self, nc, tag):
        self.nc = nc
        self.tag = tag
        self.ops = {e: [] for e in self.ENGS}
        self.cnt = {}
        self.sems = {}
        self.waited = {e: {} for e in self.ENGS}
        self.last_rd = {}
        for k in ("mm", "vx", "ac", "sd", "gd"):
            name = tag + k
            self.sems[name] = nc.semaphore(name).__enter__()
            self.cnt[name] = 0
        self.MM, self.VX, self.AC, self.SD, self.GD = (
            tag + "mm", tag + "vx", tag + "ac", tag + "sd", tag + "gd")

    def wait(self, eng, name, val):
        if val is None or val <= 0:
            return
        assert val <= self.cnt[name], f"wait on future {name}:{val} (cnt={self.cnt[name]})"
        if self.waited[eng].get(name, 0) >= val:
            return
        self.waited[eng][name] = val
        s = self.sems[name]
        self.ops[eng].append(lambda e, s=s, v=val: e.wait_ge(s, v))

    def op(self, eng, fn, inc=None, by=1):
        if inc is None:
            self.ops[eng].append(fn)
            return None
        s = self.sems[inc]
        self.cnt[inc] += by
        v = self.cnt[inc]
        self.ops[eng].append(lambda e, fn=fn, s=s, by=by: fn(e).then_inc(s, by))
        return v

    def rd(self, res, kind, val):
        d = self.last_rd.setdefault(res, {})
        if val is not None and val > d.get(kind, 0):
            d[kind] = val

    def guard(self, eng, res):
        for kind, val in self.last_rd.get(res, {}).items():
            self.wait(eng, kind, val)

    def emit(self):
        with self.nc.Block() as blk:
            @blk.tensor
            def _(t):
                for f in self.ops["t"]:
                    f(t)

            @blk.vector
            def _(v):
                for f in self.ops["v"]:
                    f(v)

            @blk.scalar
            def _(a):
                for f in self.ops["a"]:
                    f(a)

            @blk.gpsimd
            def _(g):
                for f in self.ops["g"]:
                    f(g)

            @blk.sync
            def _(s):
                for f in self.ops["s"]:
                    f(s)


def build_program():
    nc = bass.Bass("TRN2", target_bir_lowering=False)

    x_in = nc.dram_tensor("x_in", [T, H], F32, kind="ExternalInput").ap()
    wq_d = nc.dram_tensor("wq", [H, H], F32, kind="ExternalInput").ap()
    wkv_d = nc.dram_tensor("wkv", [H, 2 * H], F32, kind="ExternalInput").ap()
    wo_d = nc.dram_tensor("wo", [H, H], F32, kind="ExternalInput").ap()
    bo_d = nc.dram_tensor("bo2d", [8, 128], F32, kind="ExternalInput").ap()
    gw_d = nc.dram_tensor("gw", [H, NE], F32, kind="ExternalInput").ap()
    wg_d = nc.dram_tensor("wg", [NE, H, 2048], F32, kind="ExternalInput").ap()
    wu_d = nc.dram_tensor("wu", [NE, H, 2048], F32, kind="ExternalInput").ap()
    wd_d = nc.dram_tensor("wd", [NE, 2048, H], F32, kind="ExternalInput").ap()
    wsg_d = nc.dram_tensor("wsg", [H, 2048], F32, kind="ExternalInput").ap()
    wsu_d = nc.dram_tensor("wsu", [H, 2048], F32, kind="ExternalInput").ap()
    wsd_d = nc.dram_tensor("wsd", [2048, H], F32, kind="ExternalInput").ap()
    y_out = nc.dram_tensor("y_out", [T, H], F32, kind="ExternalOutput").ap()
    kind_sc = "ExternalOutput" if _DEBUG_XS else "Internal"
    xs1 = nc.dram_tensor("xs1", [8, 128, T], F32, kind=kind_sc).ap()
    xs2 = nc.dram_tensor("xs2", [8, 128, T], F32, kind=kind_sc).ap()

    ident_d = nc.inline_tensor(np.eye(128, dtype=np.float32), name="ident").ap()
    sel_np = np.zeros((8, 8 * 128), dtype=np.float32)
    for e in range(8):
        sel_np[e, e * 128:(e + 1) * 128] = 1.0
    sel_d = nc.inline_tensor(sel_np, name="sel").ap()
    ones_d = nc.inline_tensor(np.ones((128, 128), dtype=np.float32), name="ones").ap()

    ctx = contextlib.ExitStack()
    ec = ctx.enter_context
    with ctx:
        ident_sb = ec(nc.sbuf_tensor("ident_sb", [128, 128], F32))
        ones_b = ec(nc.sbuf_tensor("ones_b", [128, 128], BF16))
        zeros_b = ec(nc.sbuf_tensor("zeros_b", [128, 16], BF16))
        sel_sb = ec(nc.sbuf_tensor("sel_sb", [8, 1024], F32))
        gw_f = ec(nc.sbuf_tensor("gw_f", [128, 8 * 8], F32))
        sumg_bc = ec(nc.sbuf_tensor("sumg_bc", [128, 8], F32))
        sumg64 = ec(nc.sbuf_tensor("sumg64", [128, 64], F32))
        bo_sb = ec(nc.sbuf_tensor("bo_sb", [128, 8], F32))
        half_c = ec(nc.sbuf_tensor("half_c", [128, 1024], F32))
        cm12 = ec(nc.sbuf_tensor("cm12", [128, 1], F32))
        ceps = ec(nc.sbuf_tensor("ceps", [128, 1], F32))

        with (nc.semaphore("su_d") as su_d, nc.semaphore("su_c") as su_c,
              nc.psum_tensor("su_ps", [128, 512], F32) as su_ps,
              nc.sbuf_tensor("su_tmp", [8, 128], F32) as su_tmp,
              nc.sbuf_tensor("ones_f", [128, 128], F32) as ones_f,
              nc.sbuf_tensor("su_col", [8, 1], F32) as su_col,
              nc.sbuf_tensor("su_row", [1, 8], F32) as su_row,
              nc.Block() as blk):

            @blk.gpsimd
            def _(g):
                g.memset(zeros_b[:, :], 0.0)
                g.memset(half_c[:, :], 0.5)
                g.memset(cm12[:, :], -12.0)
                g.memset(ceps[:, :], 1e-5)
                g.dma_start(ones_b[:, :], ones_d).then_inc(su_d, 16)
                g.dma_start(gw_f.ap().rearrange("p (kk e) -> p kk e", kk=8),
                            gw_d.rearrange("(kk p) e -> p kk e", p=128)).then_inc(su_d, 16)

            @blk.sync
            def _(s):
                s.dma_start(ident_sb[:, :], ident_d).then_inc(su_d, 16)
                s.dma_start(ones_f[:, :], ones_d).then_inc(su_d, 16)
                s.dma_start(sel_sb[:, :], sel_d).then_inc(su_d, 16)
                s.dma_start(su_tmp[:, :], bo_d).then_inc(su_d, 16)

            @blk.tensor
            def _(t):
                t.wait_ge(su_d, 16 * 6)
                t.transpose(su_ps[:, 0:8], su_tmp[:, :], ident_sb[0:8, 0:8])
                for k in range(7):
                    t.matmul(su_ps[0:8, 16:17], gw_f[:, k * 8:(k + 1) * 8], ones_f[:, 0:1],
                             start=(k == 0), stop=False)
                t.matmul(su_ps[0:8, 16:17], gw_f[:, 56:64], ones_f[:, 0:1],
                         start=False, stop=True).then_inc(su_c, 1)

            @blk.vector
            def _(v):
                v.wait_ge(su_c, 1)
                v.tensor_copy(bo_sb[:, :], su_ps[:, 0:8]).then_inc(su_c, 1)
                v.tensor_copy(su_col[:, :], su_ps[0:8, 16:17]).then_inc(su_c, 1)

            @blk.tensor
            def _(t):
                t.wait_ge(su_c, 3)
                t.transpose(su_ps[0:1, 32:40], su_col[:, :], ident_sb[0:8, 0:8]).then_inc(su_c, 1)

            @blk.scalar
            def _(sc):
                sc.wait_ge(su_c, 4)
                sc.copy(su_row[:, :], su_ps[0:1, 32:40]).then_inc(su_c, 1)

            @blk.tensor
            def _(t):
                t.wait_ge(su_c, 5)
                t.matmul(su_ps[:, 48:56], ones_f[0:1, :], su_row[:, :],
                         start=True, stop=True).then_inc(su_c, 1)

            @blk.vector
            def _(v):
                v.wait_ge(su_c, 6)
                v.tensor_copy(sumg_bc[:, :], su_ps[:, 48:56]).then_inc(su_c, 1)
                v.wait_ge(su_c, 7)
                for b8 in range(8):
                    v.tensor_copy(sumg64[:, b8 * 8:b8 * 8 + 8], su_ps[:, 48:56])

        def ffn_phase(tag, src, dst, src_token_major, dst_token_major):
            pctx = contextlib.ExitStack()
            pec = pctx.enter_context
            with pctx:
                accs = [pec(nc.sbuf_tensor(f"{tag}_acc{i}", [128, 8 * PT], F32)) for i in range(2)]
                xln = pec(nc.sbuf_tensor(f"{tag}_xln", [128, 8 * PT], BF16))
                hid = pec(nc.sbuf_tensor(f"{tag}_hid", [128, 16 * PT], BF16))
                wgb = [pec(nc.sbuf_tensor(f"{tag}_wg{i}", [128, 8 * 256], BF16)) for i in range(2)]
                wub = [pec(nc.sbuf_tensor(f"{tag}_wu{i}", [128, 8 * 256], BF16)) for i in range(2)]
                wdb = [pec(nc.sbuf_tensor(f"{tag}_wd{i}", [128, 8 * 256], BF16)) for i in range(4)]
                s_sb = [pec(nc.sbuf_tensor(f"{tag}_s{i}", [128, 512], F32)) for i in range(2)]
                tmp_sb = [pec(nc.sbuf_tensor(f"{tag}_tmp{i}", [128, 512], F32)) for i in range(2)]
                w_bc = pec(nc.sbuf_tensor(f"{tag}_wbc", [128, 1024], F32))
                w_all = pec(nc.sbuf_tensor(f"{tag}_wall", [8, 1024], F32))
                mean_sb = pec(nc.sbuf_tensor(f"{tag}_mean", [128, 1024], F32))
                rstd_sb = pec(nc.sbuf_tensor(f"{tag}_rstd", [128, 1024], F32))
                var_sb = pec(nc.sbuf_tensor(f"{tag}_var", [128, 1024], F32))
                xb_bf = [pec(nc.sbuf_tensor(f"{tag}_xb{i}", [128, 512], BF16)) for i in range(4)]
                x2_bf = [pec(nc.sbuf_tensor(f"{tag}_x2{i}", [128, 512], BF16)) for i in range(4)]
                stage = [pec(nc.sbuf_tensor(f"{tag}_st{i}", [128, 1024], F32)) for i in range(2)]
                r1_sb = pec(nc.sbuf_tensor(f"{tag}_r1", [8, 1024], F32))
                mtok8 = pec(nc.sbuf_tensor(f"{tag}_mtok", [128, 8], F32))
                rtok8 = pec(nc.sbuf_tensor(f"{tag}_rtok", [128, 8], F32))
                rt_all = pec(nc.sbuf_tensor(f"{tag}_rtall", [128, 64], F32))
                rl = [pec(nc.sbuf_tensor(f"{tag}_rl{i}", [128, 64], F32)) for i in range(3)]
                r8 = [pec(nc.sbuf_tensor(f"{tag}_r8{i}", [128, 8], F32)) for i in range(2)]
                wtok_all = pec(nc.sbuf_tensor(f"{tag}_wtall", [128, 64], F32))
                pg = [pec(nc.psum_tensor(f"{tag}_pg{i}", [128, 512], F32)) for i in range(2)]
                pu = [pec(nc.psum_tensor(f"{tag}_pu{i}", [128, 512], F32)) for i in range(2)]
                pd = [pec(nc.psum_tensor(f"{tag}_pd{i}", [128, 512], F32)) for i in range(2)]
                paux = pec(nc.psum_tensor(f"{tag}_paux", [128, 1024], F32))

                w = Walk(nc, tag)
                MM, VX, AC, SD, GD = w.MM, w.VX, w.AC, w.SD, w.GD
                ev = {}

                weight_mats = [(wg_d[e], wu_d[e], wd_d[e]) for e in range(NE)]
                weight_mats.append((wsg_d, wsu_d, wsd_d))

                NQ_TOT = NPASS * (NE + 1) * 8
                NH_TOT = NPASS * (NE + 1) * 4
                ev_qdma = {}
                ev_hdma = {}

                def emit_q(j):
                    if j >= NQ_TOT or j in ev_qdma:
                        return
                    e_ = (j // 8) % (NE + 1)
                    o_ = j % 8
                    bidx = j % 2
                    wg_s, wu_s, _ = weight_mats[e_]
                    w.guard("g", f"wgu{bidx}")
                    for (srcm, dstb) in ((wg_s, wgb[bidx]), (wu_s, wub[bidx])):
                        vd = w.op("g", lambda g, srcm=srcm, dstb=dstb, o_=o_: g.dma_start(
                            dstb.ap().rearrange("x (kk n) -> x kk n", kk=8),
                            srcm.rearrange("(kk x) d -> x kk d", x=128)[:, :, o_ * 256:(o_ + 1) * 256]),
                            inc=GD, by=16)
                    ev_qdma[j] = vd

                def emit_hq(j):
                    if j >= NH_TOT or j in ev_hdma:
                        return
                    e_ = (j // 4) % (NE + 1)
                    hq4 = j % 4
                    bpair = (j % 2) * 2
                    wd_s = weight_mats[e_][2]
                    w.guard("g", f"wd{bpair}")
                    for k2h in range(2):
                        vd = w.op("g", lambda g, wd_s=wd_s, hq4=hq4, k2h=k2h, bpair=bpair: g.dma_start(
                            wdb[bpair + k2h].ap().rearrange("x (kk n) -> x kk n", kk=8),
                            wd_s.rearrange("(kk x) n -> x kk n", x=128)[
                                :, 8 * k2h:8 * (k2h + 1), hq4 * 256:(hq4 + 1) * 256]),
                            inc=GD, by=16)
                    ev_hdma[j] = vd

                for j in range(2):
                    emit_q(j)
                    emit_hq(j)

                for p in range(NPASS):
                    accP = accs[p % 2]
                    accR = f"acc{p % 2}"
                    if src_token_major:
                        for r in range(8):
                            st_buf = stage[r % 2]
                            w.guard("s", f"stage{r % 2}")
                            rows = p * PT + r * 128
                            vd = w.op("s", lambda s, st_buf=st_buf, rows=rows: s.dma_start(
                                st_buf[:, :], src[rows:rows + 128, :]), inc=SD, by=16)
                            w.wait("t", SD, vd)
                            w.guard("t", "paux")
                            for f in range(8):
                                def trf(t, st_buf=st_buf, f=f):
                                    return t.transpose(paux[:, f * 128:(f + 1) * 128],
                                                       st_buf[:, f * 128:(f + 1) * 128], ident_sb[:, :])
                                if f < 7:
                                    w.op("t", trf)
                                else:
                                    etr = w.op("t", trf, inc=MM)
                            w.rd(f"stage{r % 2}", MM, etr)
                            w.wait("v", MM, etr)
                            w.guard("v", accR)
                            acc_v = accP.ap().rearrange("q (kk n) -> q kk n", kk=8)[:, :, r * 128:(r + 1) * 128]
                            pa_v = paux.ap().rearrange("q (kk n) -> q kk n", kk=8)
                            eev = w.op("v", lambda v, acc_v=acc_v, pa_v=pa_v: v.tensor_copy(acc_v, pa_v),
                                       inc=VX)
                            w.rd("paux", VX, eev)
                        ing_done = ("v", VX, eev)
                    else:
                        for k in range(8):
                            w.guard("s", accR)
                            vd = w.op("s", lambda s, k=k, p=p, accP=accP: s.dma_start(
                                accP[:, k * PT:(k + 1) * PT], src[k, :, p * PT:(p + 1) * PT]),
                                inc=SD, by=16)
                        ing_done = ("s", SD, vd)

                    ie0, isem0, ival0 = ing_done
                    for c in range(2):
                        w.wait("t", isem0, ival0)
                        w.guard("t", f"pdbank{c}")
                        for k in range(8):
                            def r1mm(t, k=k, c=c, accP=accP):
                                return t.matmul(pd[c][0:8, 0:512],
                                               gw_f[:, k * 8:(k + 1) * 8],
                                               accP[:, k * PT + c * 512: k * PT + c * 512 + 512],
                                               start=(k == 0), stop=(k == 7))
                            er1 = w.op("t", r1mm, inc=(MM if k == 7 else None))
                        w.rd(accR, MM, er1)
                        w.wait("a", MM, er1)
                        w.guard("a", "r1_sb")
                        er1e = w.op("a", lambda a, c=c: a.copy(
                            r1_sb[:, c * 512:(c + 1) * 512], pd[c][0:8, 0:512]), inc=AC)
                        ev[("r1ev", p, c)] = er1e
                        w.rd(f"pdbank{c}", AC, er1e)

                    ie, isem, ival = ing_done
                    for c in range(2):
                        for k in range(8):
                            i = c * 8 + k
                            bi = i % 4
                            if ie != "v":
                                w.wait("v", isem, ival)
                            w.wait("a", isem, ival)
                            w.guard("v", f"xb{bi}")
                            w.guard("a", f"x2{bi}")
                            asl = accP[:, k * PT + c * 512: k * PT + c * 512 + 512]
                            evb = w.op("v", lambda v, b=xb_bf[bi], asl=asl: v.tensor_copy(b[:, :], asl),
                                       inc=VX)
                            evs = w.op("a", lambda a, b=x2_bf[bi], asl=asl: a.activation(
                                b[:, :], asl, AF.Square), inc=AC)
                            w.rd(accR, VX, evb)
                            w.rd(accR, AC, evs)
                            w.wait("t", VX, evb)
                            w.wait("t", AC, evs)
                            if k == 0:
                                w.guard("t", "paux")
                            w.op("t", lambda t, b=xb_bf[bi], k=k: t.matmul(
                                paux[:, 0:512], ones_b[:, :], b[:, :], start=(k == 0), stop=(k == 7)))
                            def sqm(t, b=x2_bf[bi], k=k):
                                return t.matmul(paux[:, 512:1024], ones_b[:, :], b[:, :],
                                                start=(k == 0), stop=(k == 7))
                            smm = w.op("t", sqm, inc=MM)
                            if k == 0 and c == 0:
                                w.guard("t", "pdbank0")
                                w.op("t", lambda t: t.matmul(
                                    pd[0][:, 0:16], ones_b[:, :], zeros_b[:, :],
                                    start=True, stop=True))
                            for j in range(4):
                                g = c * 4 + j
                                w.op("t", lambda t, b=xb_bf[bi], j=j, g=g, k=k: t.matmul(
                                    pd[0][:, g:g + 1], b[:, j * 128:j * 128 + 128], ones_b[:, 0:1],
                                    start=False, stop=(k == 7)))
                                def tsq(t, b=x2_bf[bi], j=j, g=g, k=k):
                                    return t.matmul(
                                        pd[0][:, 8 + g:9 + g], b[:, j * 128:j * 128 + 128],
                                        ones_b[:, 0:1], start=False, stop=(k == 7))
                                smm = w.op("t", tsq, inc=(MM if j == 3 else None))
                            w.rd(f"xb{bi}", MM, smm)
                            w.rd(f"x2{bi}", MM, smm)
                            if k == 7:
                                ev[("tokstat", p, c)] = smm
                        w.wait("a", MM, smm)
                        w.guard("a", f"mean{c}")
                        ma = w.op("a", lambda a, c=c: a.activation(
                            mean_sb[:, c * 512:(c + 1) * 512], paux[:, 0:512], AF.Copy,
                            scale=1.0 / H), inc=AC)
                        mb = w.op("a", lambda a, c=c: a.activation(
                            var_sb[:, c * 512:(c + 1) * 512], paux[:, 512:1024], AF.Copy,
                            scale=1.0 / H), inc=AC)
                        w.rd("paux", AC, mb)
                        w.wait("v", AC, mb)
                        w.guard("v", f"rstd{c}")
                        e1 = w.op("v", lambda v, c=c: v.tensor_mul(
                            rstd_sb[:, c * 512:(c + 1) * 512], mean_sb[:, c * 512:(c + 1) * 512],
                            mean_sb[:, c * 512:(c + 1) * 512]), inc=VX)
                        w.wait("v", VX, e1)
                        e2 = w.op("v", lambda v, c=c: v.tensor_sub(
                            var_sb[:, c * 512:(c + 1) * 512], var_sb[:, c * 512:(c + 1) * 512],
                            rstd_sb[:, c * 512:(c + 1) * 512]), inc=VX)
                        w.wait("a", VX, e2)
                        e3 = w.op("a", lambda a, c=c: a.activation(
                            rstd_sb[:, c * 512:(c + 1) * 512], var_sb[:, c * 512:(c + 1) * 512],
                            AF.Sqrt, bias=ceps[:, 0:1]), inc=AC)
                        w.wait("v", AC, e3)
                        ev[("rstd", p, c)] = w.op("v", lambda v, c=c: v.reciprocal(
                            rstd_sb[:, c * 512:(c + 1) * 512], rstd_sb[:, c * 512:(c + 1) * 512]),
                            inc=VX)

                    w.wait("a", MM, ev[("tokstat", p, 1)])
                    w.guard("a", "mtok8")
                    tm1 = w.op("a", lambda a: a.activation(
                        mtok8[:, :], pd[0][:, 0:8], AF.Copy, scale=1.0 / H), inc=AC)
                    tm2 = w.op("a", lambda a: a.activation(
                        rtok8[:, :], pd[0][:, 8:16], AF.Copy, scale=1.0 / H), inc=AC)
                    w.rd("pdbank0", AC, tm2)
                    w.wait("v", AC, tm2)
                    w.guard("v", "r8_0")
                    tm3 = w.op("v", lambda v: v.tensor_mul(r8[0][:, :], mtok8[:, :], mtok8[:, :]), inc=VX)
                    w.wait("v", VX, tm3)
                    tm4 = w.op("v", lambda v: v.tensor_sub(rtok8[:, :], rtok8[:, :], r8[0][:, :]), inc=VX)
                    w.wait("a", VX, tm4)
                    tm5 = w.op("a", lambda a: a.activation(
                        rtok8[:, :], rtok8[:, :], AF.Sqrt, bias=ceps[:, 0:1]), inc=AC)
                    w.wait("v", AC, tm5)
                    ev[("rtok", p)] = w.op("v", lambda v: v.reciprocal(rtok8[:, :], rtok8[:, :]), inc=VX)

                    for k in range(8):
                        for c in range(2):
                            w.wait("v", VX, ev[("rstd", p, c)])
                            w.guard("v", "xln")
                            w.guard("v", f"tmp{(k * 2 + c) % 2}")
                            asl = accP[:, k * PT + c * 512: k * PT + c * 512 + 512]
                            msl = mean_sb[:, c * 512:(c + 1) * 512]
                            rsl = rstd_sb[:, c * 512:(c + 1) * 512]
                            tb = tmp_sb[(k * 2 + c) % 2]
                            e1 = w.op("v", lambda v, tb=tb, asl=asl, msl=msl: v.tensor_sub(
                                tb[:, :], asl, msl), inc=VX)
                            w.wait("v", VX, e1)
                            w.rd(accR, VX, e1)
                            xsl = xln[:, k * PT + c * 512: k * PT + c * 512 + 512]
                            e2 = w.op("v", lambda v, xsl=xsl, tb=tb, rsl=rsl: v.tensor_mul(
                                xsl, tb[:, :], rsl), inc=VX)
                            w.rd(f"mean{c}", VX, e1)
                            w.rd(f"rstd{c}", VX, e2)
                            w.rd(f"tmp{(k * 2 + c) % 2}", VX, e2)
                    norm_done = e2

                    w.guard("t", "paux")
                    w.wait("t", AC, ev[("r1ev", p, 1)])
                    w.wait("t", VX, ev[("rstd", p, 1)])
                    for bi_ in range(8):
                        def rtr(t, bi_=bi_):
                            return t.transpose(
                                paux[:, bi_ * 8:bi_ * 8 + 8], r1_sb[:, bi_ * 128:(bi_ + 1) * 128],
                                ident_sb[0:8, 0:8])
                        et3 = w.op("t", rtr, inc=(MM if bi_ == 7 else None))
                    w.rd("r1_sb", MM, et3)
                    def b3(tens, bstride, estride):
                        pap = tens.ap()
                        return bass.AP(tens, 0, [list(pap.ap[0]), [bstride, 8], [estride, 8]])
                    w.wait("v", MM, et3)
                    w.wait("v", VX, ev[("rtok", p)])
                    c0 = w.op("v", lambda v: v.tensor_copy(rt_all[:, :], paux[:, 0:64]), inc=VX)
                    w.rd("paux", VX, c0)
                    w.wait("v", VX, c0)
                    e1 = w.op("v", lambda v: v.tensor_tensor(
                        b3(rl[0], 8, 1), b3(sumg64, 8, 1), b3(mtok8, 1, 0), ALU.mult), inc=VX)
                    w.rd("mtok8", VX, e1)
                    w.wait("v", VX, e1)
                    e2 = w.op("v", lambda v: v.tensor_sub(rl[0][:, :], rt_all[:, :], rl[0][:, :]), inc=VX)
                    w.wait("v", VX, e2)
                    e3 = w.op("v", lambda v: v.tensor_tensor(
                        b3(rl[0], 8, 1), b3(rl[0], 8, 1), b3(rtok8, 1, 0), ALU.mult), inc=VX)
                    w.rd("mtok8", VX, e3)
                    w.rd("r8_0", VX, e3)
                    w.wait("v", VX, e3)
                    e4 = w.op("v", lambda v: v.reduce_max(
                        r8[0][:, :], rl[0].ap().rearrange("p (b e) -> p b e", b=8), AX.X,
                        negate=True), inc=VX)
                    w.wait("v", VX, e4)
                    e5 = w.op("v", lambda v: v.tensor_tensor(
                        b3(rl[2], 8, 1), b3(rl[0], 8, 1), b3(r8[0], 1, 0), ALU.add), inc=VX)
                    w.wait("a", VX, e5)
                    e6 = w.op("a", lambda a: a.activation(rl[1][:, :], rl[2][:, :], AF.Exp), inc=AC)
                    w.wait("v", AC, e6)
                    e7 = w.op("v", lambda v: v.reduce_sum(
                        r8[0][:, :], rl[1].ap().rearrange("p (b e) -> p b e", b=8), AX.X), inc=VX)
                    w.wait("v", VX, e7)
                    e8 = w.op("v", lambda v: v.reciprocal(r8[0][:, :], r8[0][:, :]), inc=VX)
                    w.wait("v", VX, e8)
                    e9 = w.op("v", lambda v: v.tensor_tensor(
                        b3(rl[1], 8, 1), b3(rl[1], 8, 1), b3(r8[0], 1, 0), ALU.mult), inc=VX)
                    w.wait("v", VX, e9)
                    e10 = w.op("v", lambda v: v.reduce_max(
                        r8[0][:, :], rl[1].ap().rearrange("p (b e) -> p b e", b=8), AX.X), inc=VX)
                    w.wait("v", VX, e10)
                    e11 = w.op("v", lambda v: v.tensor_tensor(
                        b3(rl[2], 8, 1), b3(rl[1], 8, 1), b3(r8[0], 1, 0), ALU.is_ge), inc=VX)
                    w.wait("v", VX, e11)
                    e12 = w.op("v", lambda v: v.scalar_tensor_tensor(
                        rl[2][:, :], rl[2][:, :], -1e30, rl[1][:, :], ALU.mult, ALU.add), inc=VX)
                    w.wait("v", VX, e12)
                    e13 = w.op("v", lambda v: v.reduce_max(
                        r8[0][:, :], rl[2].ap().rearrange("p (b e) -> p b e", b=8), AX.X), inc=VX)
                    w.wait("v", VX, e13)
                    e14 = w.op("v", lambda v: v.tensor_tensor(
                        b3(rl[2], 8, 1), b3(rl[1], 8, 1), b3(r8[0], 1, 0), ALU.is_ge), inc=VX)
                    w.wait("v", VX, e14)
                    e15 = w.op("v", lambda v: v.tensor_mul(rl[2][:, :], rl[2][:, :], rl[1][:, :]), inc=VX)
                    w.wait("v", VX, e15)
                    e16 = w.op("v", lambda v: v.reduce_sum(
                        r8[1][:, :], rl[2].ap().rearrange("p (b e) -> p b e", b=8), AX.X), inc=VX)
                    w.wait("v", VX, e16)
                    e17 = w.op("v", lambda v: v.tensor_scalar(
                        r8[1][:, :], r8[1][:, :], 2.0, 2e-20, ALU.mult, ALU.add), inc=VX)
                    w.wait("v", VX, e17)
                    e18 = w.op("v", lambda v: v.reciprocal(r8[1][:, :], r8[1][:, :]), inc=VX)
                    w.wait("v", VX, e18)
                    w.guard("v", "wtok")
                    e19 = w.op("v", lambda v: v.tensor_tensor(
                        b3(wtok_all, 8, 1), b3(rl[2], 8, 1), b3(r8[1], 1, 0), ALU.mult), inc=VX)
                    w.wait("t", VX, e19)
                    for bi_ in range(8):
                        reg = 64 + (bi_ % 2) * 128
                        w.wait("t", AC, ev.get(("wall", p, bi_ - 2), 0))
                        ewt = w.op("t", lambda t, bi_=bi_, reg=reg: t.transpose(
                            paux[0:8, reg:reg + 128], wtok_all[:, bi_ * 8:bi_ * 8 + 8],
                            ident_sb[:, :]), inc=MM)
                        w.wait("a", MM, ewt)
                        w.guard("a", "w_all")
                        ewe = w.op("a", lambda a, bi_=bi_, reg=reg: a.copy(
                            w_all[:, bi_ * 128:(bi_ + 1) * 128], paux[0:8, reg:reg + 128]), inc=AC)
                        ev[("wall", p, bi_)] = ewe
                    w.rd("wtok", MM, ewt)
                    w.rd("paux", AC, ewe)
                    router_done = ev[("wall", p, 7)]

                    for e in range(NE + 1):
                        shared = (e == NE)
                        pe_idx = p * (NE + 1) + e
                        if not shared:
                            w.wait("t", AC, router_done)
                            for c in range(2):
                                w.guard("t", "paux")
                                ewm = w.op("t", lambda t, e=e, c=c: t.matmul(
                                    paux[:, c * 512:(c + 1) * 512],
                                    sel_sb[:, e * 128:(e + 1) * 128],
                                    w_all[:, c * 512:(c + 1) * 512],
                                    start=True, stop=True), inc=MM)
                                w.rd("w_all", MM, ewm)
                                w.wait("a", MM, ewm)
                                w.guard("a", f"wbc{c}")
                                ewc = w.op("a", lambda a, c=c: a.copy(
                                    w_bc[:, c * 512:(c + 1) * 512], paux[:, c * 512:(c + 1) * 512]),
                                    inc=AC)
                                w.rd("paux", AC, ewc)
                                ev[("wbc", p, e, c)] = ewc
                            wbc_t = w_bc
                        else:
                            wbc_t = half_c

                        for d in range(16):
                            oct_ = d // 2
                            jq = pe_idx * 8 + oct_
                            bidx = jq % 2
                            do = d % 2
                            for c in range(2):
                                w.wait("t", GD, ev_qdma[jq])
                                w.wait("t", VX, norm_done)
                                w.guard("t", f"pgbank{c}")
                                for k in range(8):
                                    def gmm(t, bidx=bidx, do=do, k=k, c=c):
                                        return t.matmul(
                                            pg[c][:, :],
                                            wgb[bidx][:, k * 256 + do * 128: k * 256 + do * 128 + 128],
                                            xln[:, k * PT + c * 512: k * PT + c * 512 + 512],
                                            start=(k == 0), stop=(k == 7))
                                    egm = w.op("t", gmm, inc=(MM if k == 7 else None))
                                w.guard("t", f"pubank{c}")
                                for k in range(8):
                                    def umm(t, bidx=bidx, do=do, k=k, c=c):
                                        return t.matmul(
                                            pu[c][:, :],
                                            wub[bidx][:, k * 256 + do * 128: k * 256 + do * 128 + 128],
                                            xln[:, k * PT + c * 512: k * PT + c * 512 + 512],
                                            start=(k == 0), stop=(k == 7))
                                    eum = w.op("t", umm, inc=(MM if k == 7 else None))
                                w.rd("xln", MM, eum)
                                w.rd(f"wgu{bidx}", MM, eum)
                                w.wait("a", MM, egm)
                                w.guard("a", f"s{c}")
                                esl = w.op("a", lambda a, c=c: a.activation(
                                    s_sb[c][:, :], pg[c][:, :], AF.Silu), inc=AC)
                                w.rd(f"pgbank{c}", AC, esl)
                                w.wait("v", MM, eum)
                                w.wait("v", AC, esl)
                                w.guard("v", f"hid{c}")
                                hsl = hid[:, d * PT + c * 512: d * PT + c * 512 + 512]
                                eml = w.op("v", lambda v, hsl=hsl, c=c: v.tensor_mul(
                                    hsl, s_sb[c][:, :], pu[c][:, :]), inc=VX)
                                w.rd(f"pubank{c}", VX, eml)
                                w.rd(f"s{c}", VX, eml)
                                ev[("mult", p, e, d, c)] = eml
                            if d % 2 == 1:
                                emit_q(pe_idx * 8 + d // 2 + 2)

                        for h in range(8):
                            hq4 = h // 2
                            jh = pe_idx * 4 + hq4
                            bpair = (jh % 2) * 2
                            ho = h % 2
                            for c in range(2):
                                w.wait("t", GD, ev_hdma[jh])
                                w.wait("t", VX, ev[("mult", p, e, 15, c)])
                                w.guard("t", f"pdbank{c}")
                                for k2 in range(16):
                                    def dmm(t, bpair=bpair, k2=k2, ho=ho, c=c):
                                        return t.matmul(
                                            pd[c][:, :],
                                            wdb[bpair + k2 // 8][:, (k2 % 8) * 256 + ho * 128:
                                                                 (k2 % 8) * 256 + ho * 128 + 128],
                                            hid[:, k2 * PT + c * 512: k2 * PT + c * 512 + 512],
                                            start=(k2 == 0), stop=(k2 == 15))
                                    edm = w.op("t", dmm, inc=(MM if k2 == 15 else None))
                                w.rd(f"hid{c}", MM, edm)
                                w.rd(f"wd{bpair}", MM, edm)
                                w.wait("v", MM, edm)
                                if not shared:
                                    w.wait("v", AC, ev[("wbc", p, e, c)])
                                tb = tmp_sb[c]
                                w.guard("v", f"tmp{c}")
                                etm = w.op("v", lambda v, tb=tb, c=c, wbc_t=wbc_t: v.tensor_mul(
                                    tb[:, :], wbc_t[:, c * 512:(c + 1) * 512], pd[c][:, :]), inc=VX)
                                w.rd(f"pdbank{c}", VX, etm)
                                if not shared:
                                    w.rd(f"wbc{c}", VX, etm)
                                w.wait("v", VX, etm)
                                asl = accP[:, h * PT + c * 512: h * PT + c * 512 + 512]
                                ead = w.op("v", lambda v, asl=asl, tb=tb: v.tensor_add(
                                    asl, asl, tb[:, :]), inc=VX)
                                w.rd(f"tmp{c}", VX, ead)
                                ev[("accadd", p, e, h, c)] = ead
                            if h % 2 == 1:
                                emit_hq(pe_idx * 4 + h // 2 + 2)
                    if dst_token_major:
                        for r in range(8):
                            c = r // 4
                            ob = stage[r % 2]
                            w.wait("t", VX, ev[("accadd", p, NE, 7, c)])
                            w.guard("t", "paux")
                            for f in range(8):
                                def etr2(t, f=f, r=r, accP=accP):
                                    return t.transpose(paux[:, f * 128:(f + 1) * 128],
                                                       accP[:, f * PT + r * 128: f * PT + r * 128 + 128],
                                                       ident_sb[:, :])
                                etr = w.op("t", etr2, inc=(MM if f == 7 else None))
                            w.rd(accR, MM, etr)
                            w.wait("v", MM, etr)
                            w.guard("v", f"stage{r % 2}")
                            eev = w.op("v", lambda v, ob=ob: v.tensor_copy(ob[:, :], paux[:, :]), inc=VX)
                            w.rd("paux", VX, eev)
                            w.wait("s", VX, eev)
                            rows = p * PT + r * 128
                            edr = w.op("s", lambda s, ob=ob, rows=rows: s.dma_start(
                                dst[rows:rows + 128, :], ob[:, :]), inc=SD, by=16)
                            w.rd(f"stage{r % 2}", SD, edr)
                    else:
                        for k in range(8):
                            w.wait("s", VX, ev[("accadd", p, NE, k, 1)])
                            edr = w.op("s", lambda s, k=k, p=p, accP=accP: s.dma_start(
                                dst[k, :, p * PT:(p + 1) * PT], accP[:, k * PT:(k + 1) * PT]),
                                inc=SD, by=16)
                            w.rd(accR, SD, edr)

                w.wait("s", SD, w.cnt[SD])
                w.wait("g", GD, w.cnt[GD])
                w.emit()

        def attn_phase(tag, src, dst):
            pctx = contextlib.ExitStack()
            pec = pctx.enter_context
            with pctx:
                xln2 = pec(nc.sbuf_tensor(f"{tag}_xln", [128, 8 * SEQ], BF16))
                attn = xln2
                q_bf = pec(nc.sbuf_tensor(f"{tag}_q", [128, 8 * SEQ], BF16))
                k_bf = pec(nc.sbuf_tensor(f"{tag}_k", [128, 8 * SEQ], BF16))
                vT = pec(nc.sbuf_tensor(f"{tag}_vT", [128, 16 * H], BF16))
                wbuf = [pec(nc.sbuf_tensor(f"{tag}_wb{i}", [128, 8 * 1024], BF16)) for i in range(2)]
                xc = [pec(nc.sbuf_tensor(f"{tag}_xc{i}", [128, 512], F32)) for i in range(4)]
                xbb = [pec(nc.sbuf_tensor(f"{tag}_xbb{i}", [128, 512], BF16)) for i in range(2)]
                x2b = [pec(nc.sbuf_tensor(f"{tag}_x2b{i}", [128, 512], BF16)) for i in range(2)]
                mean_sb = pec(nc.sbuf_tensor(f"{tag}_mean", [128, 1024], F32))
                rstd_sb = pec(nc.sbuf_tensor(f"{tag}_rstd", [128, 1024], F32))
                var_sb = pec(nc.sbuf_tensor(f"{tag}_var", [128, 1024], F32))
                probs = [pec(nc.sbuf_tensor(f"{tag}_pr{i}", [128, 1024], BF16)) for i in range(2)]
                inv_sb = [pec(nc.sbuf_tensor(f"{tag}_inv0", [128, 512], F32))] * 2
                t_sb = [pec(nc.sbuf_tensor(f"{tag}_t{i}", [128, 512], F32)) for i in range(2)]
                o_sb = [pec(nc.sbuf_tensor(f"{tag}_o{i}", [128, 512], F32)) for i in range(2)]
                psS = [pec(nc.psum_tensor(f"{tag}_pS{i}", [128, 1024], F32)) for i in range(2)]
                pav = pec(nc.psum_tensor(f"{tag}_pav", [128, 512], F32))
                psm = pec(nc.psum_tensor(f"{tag}_psm", [128, 512], F32))
                po = [pec(nc.psum_tensor(f"{tag}_po{i}", [128, 512], F32)) for i in range(2)]

                w = Walk(nc, tag)
                MM, VX, AC, SD, GD = w.MM, w.VX, w.AC, w.SD, w.GD
                ev = {}

                for b in range(2):
                    base = b * SEQ
                    for c in range(4):
                        cc = c % 2
                        for k in range(8):
                            i = c * 8 + k
                            xb = xc[i % 4]
                            w.guard("s", f"xc{i % 4}")
                            vd = w.op("s", lambda s, xb=xb, k=k, c=c, base=base: s.dma_start(
                                xb[:, :], src[k, :, base + c * 512: base + c * 512 + 512]),
                                inc=SD, by=16)
                            w.wait("v", SD, vd)
                            w.wait("a", SD, vd)
                            w.guard("v", f"xbb{i % 2}")
                            w.guard("a", f"x2b{i % 2}")
                            evb = w.op("v", lambda v, xb=xb, j=i % 2: v.tensor_copy(
                                xbb[j][:, :], xb[:, :]), inc=VX)
                            evs = w.op("a", lambda a, xb=xb, j=i % 2: a.activation(
                                x2b[j][:, :], xb[:, :], AF.Square), inc=AC)
                            w.rd(f"xc{i % 4}", VX, evb)
                            w.rd(f"xc{i % 4}", AC, evs)
                            w.wait("t", VX, evb)
                            w.wait("t", AC, evs)
                            if k == 0:
                                w.guard("t", "po0")
                                w.guard("t", "po1")
                            w.op("t", lambda t, j=i % 2, k=k: t.matmul(
                                po[0][:, :], ones_b[:, :], xbb[j][:, :], start=(k == 0), stop=(k == 7)))
                            smm = w.op("t", lambda t, j=i % 2, k=k: t.matmul(
                                po[1][:, :], ones_b[:, :], x2b[j][:, :], start=(k == 0), stop=(k == 7)),
                                inc=MM)
                            w.rd(f"xbb{i % 2}", MM, smm)
                            w.rd(f"x2b{i % 2}", MM, smm)
                        w.wait("a", MM, smm)
                        w.guard("a", f"mean{cc}")
                        ma = w.op("a", lambda a, cc=cc: a.activation(
                            mean_sb[:, cc * 512:(cc + 1) * 512], po[0][:, :], AF.Copy, scale=1.0 / H),
                            inc=AC)
                        mb = w.op("a", lambda a, cc=cc: a.activation(
                            var_sb[:, cc * 512:(cc + 1) * 512], po[1][:, :], AF.Copy, scale=1.0 / H),
                            inc=AC)
                        w.rd("po0", AC, mb)
                        w.rd("po1", AC, mb)
                        w.wait("v", AC, mb)
                        w.guard("v", f"rstd{cc}")
                        e1 = w.op("v", lambda v, cc=cc: v.tensor_mul(
                            rstd_sb[:, cc * 512:(cc + 1) * 512], mean_sb[:, cc * 512:(cc + 1) * 512],
                            mean_sb[:, cc * 512:(cc + 1) * 512]), inc=VX)
                        w.wait("v", VX, e1)
                        e2 = w.op("v", lambda v, cc=cc: v.tensor_sub(
                            var_sb[:, cc * 512:(cc + 1) * 512], var_sb[:, cc * 512:(cc + 1) * 512],
                            rstd_sb[:, cc * 512:(cc + 1) * 512]), inc=VX)
                        w.wait("a", VX, e2)
                        e3 = w.op("a", lambda a, cc=cc: a.activation(
                            rstd_sb[:, cc * 512:(cc + 1) * 512], var_sb[:, cc * 512:(cc + 1) * 512],
                            AF.Sqrt, bias=ceps[:, 0:1]), inc=AC)
                        w.wait("v", AC, e3)
                        erc = w.op("v", lambda v, cc=cc: v.reciprocal(
                            rstd_sb[:, cc * 512:(cc + 1) * 512], rstd_sb[:, cc * 512:(cc + 1) * 512]),
                            inc=VX)
                        for k in range(8):
                            i = c * 8 + k
                            xb = xc[i % 4]
                            w.guard("s", f"xc{i % 4}")
                            vd = w.op("s", lambda s, xb=xb, k=k, c=c, base=base: s.dma_start(
                                xb[:, :], src[k, :, base + c * 512: base + c * 512 + 512]),
                                inc=SD, by=16)
                            w.wait("v", SD, vd)
                            w.wait("v", VX, erc)
                            w.guard("v", f"t{k % 2}")
                            w.guard("v", "xln2")
                            tb = t_sb[k % 2]
                            e4 = w.op("v", lambda v, tb=tb, xb=xb, cc=cc: v.tensor_sub(
                                tb[:, :], xb[:, :], mean_sb[:, cc * 512:(cc + 1) * 512]), inc=VX)
                            w.rd(f"xc{i % 4}", VX, e4)
                            w.rd(f"mean{cc}", VX, e4)
                            w.wait("v", VX, e4)
                            xsl = xln2[:, k * SEQ + c * 512: k * SEQ + c * 512 + 512]
                            e5 = w.op("v", lambda v, xsl=xsl, tb=tb, cc=cc: v.tensor_mul(
                                xsl, tb[:, :], rstd_sb[:, cc * 512:(cc + 1) * 512]), inc=VX)
                            w.rd(f"t{k % 2}", VX, e5)
                            w.rd(f"rstd{cc}", VX, e5)
                    ln_done = e5

                    w.guard("g", "wbuf0")
                    ev_wq = w.op("g", lambda g: g.dma_start(
                        wbuf[0].ap().rearrange("x (kk n) -> x kk n", kk=8),
                        wq_d.rearrange("(kk x) n -> x kk n", x=128)), inc=GD, by=16)
                    w.guard("g", "wbuf1")
                    ev_wk = w.op("g", lambda g: g.dma_start(
                        wbuf[1].ap().rearrange("x (kk n) -> x kk n", kk=8),
                        wkv_d.rearrange("(kk x) n -> x kk n", x=128)[:, :, 0:1024]), inc=GD, by=16)

                    def proj(wb_i, out_sb, out_res, wdma_ev, evac_eng, res_tag):
                        last = None
                        for m in range(8):
                            for c in range(4):
                                i = m * 4 + c
                                pb = po[i % 2]
                                w.wait("t", GD, wdma_ev)
                                w.wait("t", VX, ln_done)
                                w.guard("t", f"po{i % 2}")
                                for k in range(8):
                                    def pmm(t, pb=pb, m=m, c=c, k=k, wb_i=wb_i):
                                        return t.matmul(
                                            pb[:, :],
                                            wbuf[wb_i][:, k * 1024 + m * 128: k * 1024 + m * 128 + 128],
                                            xln2[:, k * SEQ + c * 512: k * SEQ + c * 512 + 512],
                                            start=(k == 0), stop=(k == 7))
                                    emm = w.op("t", pmm, inc=(MM if k == 7 else None))
                                w.rd(f"wbuf{wb_i}", MM, emm)
                                w.rd("xln2", MM, emm)
                                w.wait(evac_eng, MM, emm)
                                osl = out_sb[:, m * SEQ + c * 512: m * SEQ + c * 512 + 512]
                                if evac_eng == "v":
                                    eev = w.op("v", lambda v, osl=osl, pb=pb: v.tensor_copy(osl, pb[:, :]),
                                               inc=VX)
                                    w.rd(f"po{i % 2}", VX, eev)
                                else:
                                    eev = w.op("a", lambda a, osl=osl, pb=pb: a.copy(osl, pb[:, :]),
                                               inc=AC)
                                    w.rd(f"po{i % 2}", AC, eev)
                                ev[(res_tag, b, m, c)] = eev
                                last = emm
                        return last

                    qproj_mm = proj(0, q_bf, None, ev_wq, "v", "qev")
                    kproj_mm = proj(1, k_bf, None, ev_wk, "v", "kev")

                    w.guard("g", "wbuf0")
                    ev_wv = w.op("g", lambda g: g.dma_start(
                        wbuf[0].ap().rearrange("x (kk n) -> x kk n", kk=8),
                        wkv_d.rearrange("(kk x) n -> x kk n", x=128)[:, :, 1024:2048]), inc=GD, by=16)

                    for kc in range(16):
                        for n in range(2):
                            i = kc * 2 + n
                            pb = po[i % 2]
                            w.wait("t", GD, ev_wv)
                            w.guard("t", f"po{i % 2}")
                            for k in range(8):
                                def vmm(t, pb=pb, kc=kc, n=n, k=k):
                                    return t.matmul(
                                        pb[:, :],
                                        xln2[:, k * SEQ + kc * 128: k * SEQ + kc * 128 + 128],
                                        wbuf[0][:, k * 1024 + n * 512: k * 1024 + n * 512 + 512],
                                        start=(k == 0), stop=(k == 7))
                                emm = w.op("t", vmm, inc=(MM if k == 7 else None))
                            w.rd("wbuf0", MM, emm)
                            w.rd("xln2", MM, emm)
                            w.wait("a", MM, emm)
                            w.guard("a", "vT")
                            vsl = vT[:, kc * H + n * 512: kc * H + n * 512 + 512]
                            eev = w.op("a", lambda a, vsl=vsl, pb=pb: a.copy(vsl, pb[:, :]), inc=AC)
                            w.rd(f"po{i % 2}", AC, eev)
                    vT_done = eev

                    w.guard("g", "wbuf1")
                    ev_wo = w.op("g", lambda g: g.dma_start(
                        wbuf[1].ap().rearrange("x (kk n) -> x kk n", kk=8),
                        wo_d.rearrange("(kk x) n -> x kk n", x=128)), inc=GD, by=16)

                    for hp in range(8):
                        hA = 2 * hp
                        for qg in range(4):
                            gidx = hp * 4 + qg
                            pavT = [pav, po[0]][gidx % 2]
                            psmT = [psm, po[1]][gidx % 2]
                            pavR = ["pav", "po0"][gidx % 2]
                            psmR = ["psm", "po1"][gidx % 2]
                            w.guard("t", pavR)
                            w.guard("t", psmR)

                            def emit_scores(kc, hp=hp, qg=qg):
                                j = kc % 2
                                sS = psS[j]
                                w.wait("t", VX, ev[("kev", b, hp, 3)])
                                w.wait("t", VX, ev[("qev", b, hp, 3)])
                                w.guard("t", f"psS{j}")
                                def smmA(t, sS=sS, kc=kc, hp=hp, qg=qg):
                                    return t.matmul(
                                        sS[:, 0:512],
                                        k_bf[0:64, hp * SEQ + kc * 128: hp * SEQ + kc * 128 + 128],
                                        q_bf[0:64, hp * SEQ + qg * 512: hp * SEQ + qg * 512 + 512],
                                        start=True, stop=True, tile_position=(0, 0))
                                w.op("t", smmA)
                                def smmB(t, sS=sS, kc=kc, hp=hp, qg=qg):
                                    return t.matmul(
                                        sS[:, 512:1024],
                                        k_bf[64:128, hp * SEQ + kc * 128: hp * SEQ + kc * 128 + 128],
                                        q_bf[64:128, hp * SEQ + qg * 512: hp * SEQ + qg * 512 + 512],
                                        start=True, stop=True, tile_position=(64, 0))
                                return w.op("t", smmB, inc=MM)

                            ev_s = {0: emit_scores(0)}
                            for kc in range(16):
                                j = kc % 2
                                if kc + 1 < 16:
                                    ev_s[kc + 1] = emit_scores(kc + 1)
                                pP = probs[j]
                                w.wait("a", MM, ev_s[kc])
                                w.guard("a", f"pr{j}")
                                eex = w.op("a", lambda a, pP=pP, sS=psS[j]: a.activation(
                                    pP[:, :], sS[:, :], AF.Exp, bias=cm12[:, 0:1], scale=0.125),
                                    inc=AC)
                                w.rd(f"psS{j}", AC, eex)
                                w.wait("t", AC, eex)
                                if kc == 0:
                                    w.wait("t", AC, vT_done)
                                def avA(t, pP=pP, hA=hA, kc=kc, pavT=pavT):
                                    return t.matmul(
                                        pavT[0:64, :],
                                        vT[:, kc * H + hA * 64: kc * H + hA * 64 + 64],
                                        pP[:, 0:512], start=(kc == 0), stop=(kc == 15),
                                        tile_position=(0, 0))
                                def avB(t, pP=pP, hA=hA, kc=kc, pavT=pavT):
                                    return t.matmul(
                                        pavT[64:128, :],
                                        vT[:, kc * H + hA * 64 + 64: kc * H + hA * 64 + 128],
                                        pP[:, 512:1024], start=(kc == 0), stop=(kc == 15),
                                        tile_position=(0, 64))
                                def smA(t, pP=pP, kc=kc, psmT=psmT):
                                    return t.matmul(
                                        psmT[0:64, :], ones_b[:, 0:64], pP[:, 0:512],
                                        start=(kc == 0), stop=(kc == 15), tile_position=(0, 0))
                                def smB(t, pP=pP, kc=kc, psmT=psmT):
                                    return t.matmul(
                                        psmT[64:128, :], ones_b[:, 0:64], pP[:, 512:1024],
                                        start=(kc == 0), stop=(kc == 15), tile_position=(0, 64))
                                w.op("t", avA)
                                w.op("t", avB)
                                w.op("t", smA)
                                eprd = w.op("t", smB, inc=MM)
                                w.rd(f"pr{j}", MM, eprd)
                                w.rd("vT", MM, eprd)
                            w.wait("v", MM, eprd)
                            iv = inv_sb[0]
                            w.guard("v", "inv0")
                            er = w.op("v", lambda v, iv=iv, psmT=psmT: v.reciprocal(iv[:, :], psmT[:, :]), inc=VX)
                            w.rd(psmR, VX, er)
                            w.wait("v", VX, er)
                            w.guard("v", "xln2")
                            asl = attn[:, hp * SEQ + qg * 512: hp * SEQ + qg * 512 + 512]
                            eat = w.op("v", lambda v, asl=asl, iv=iv, pavT=pavT: v.tensor_mul(
                                asl, pavT[:, :], iv[:, :]), inc=VX)
                            w.rd(pavR, VX, eat)
                            w.rd("inv0", VX, eat)
                            ev[("atev", b, hp, qg)] = eat

                    for m in range(8):
                        for c in range(4):
                            i = m * 4 + c
                            pb = po[i % 2]
                            w.wait("t", GD, ev_wo)
                            for hp in range(8):
                                w.wait("t", VX, ev[("atev", b, hp, c)])
                            w.guard("t", f"po{i % 2}")
                            for k in range(8):
                                def omm(t, pb=pb, m=m, c=c, k=k):
                                    return t.matmul(
                                        pb[:, :],
                                        wbuf[1][:, k * 1024 + m * 128: k * 1024 + m * 128 + 128],
                                        attn[:, k * SEQ + c * 512: k * SEQ + c * 512 + 512],
                                        start=(k == 0), stop=(k == 7))
                                emm = w.op("t", omm, inc=(MM if k == 7 else None))
                            w.rd("wbuf1", MM, emm)
                            w.rd("xln2", MM, emm)
                            ob = o_sb[i % 2]
                            xb = xc[i % 4]
                            w.guard("s", f"xc{i % 4}")
                            vd = w.op("s", lambda s, xb=xb, m=m, c=c, base=base: s.dma_start(
                                xb[:, :], src[m, :, base + c * 512: base + c * 512 + 512]),
                                inc=SD, by=16)
                            w.wait("v", SD, vd)
                            w.wait("v", MM, emm)
                            w.guard("v", f"o{i % 2}")
                            ead = w.op("v", lambda v, ob=ob, pb=pb, xb=xb, m=m: v.scalar_tensor_tensor(
                                ob[:, :], pb[:, :], bo_sb[:, m:m + 1], xb[:, :], ALU.add, ALU.add),
                                inc=VX)
                            w.rd(f"po{i % 2}", VX, ead)
                            w.rd(f"xc{i % 4}", VX, ead)
                            w.wait("s", VX, ead)
                            eds = w.op("s", lambda s, ob=ob, m=m, c=c, base=base: s.dma_start(
                                dst[m, :, base + c * 512: base + c * 512 + 512], ob[:, :]),
                                inc=SD, by=16)
                            w.rd(f"o{i % 2}", SD, eds)

                w.wait("s", SD, w.cnt[SD])
                w.wait("g", GD, w.cnt[GD])
                w.emit()

        ffn_phase("A", x_in, xs1, True, False)
        attn_phase("B", xs1, xs2)
        ffn_phase("C", xs2, y_out, False, True)

    return nc


_NC_CACHE = None


def kernel(L0_attn_hidden_states, L0_attn_Wq, L0_attn_Wkv, L0_attn_Wo, L0_attn_bo,
           L0_ffn_gate_w, L0_ffn_w_gate, L0_ffn_w_up, L0_ffn_w_down,
           L0_ffn_ws_gate, L0_ffn_ws_up, L0_ffn_ws_down):
    global _NC_CACHE
    x = np.ascontiguousarray(np.asarray(L0_attn_hidden_states, dtype=np.float32))
    common = {
        "wq": np.ascontiguousarray(np.asarray(L0_attn_Wq, dtype=np.float32)),
        "wkv": np.ascontiguousarray(np.asarray(L0_attn_Wkv, dtype=np.float32)),
        "wo": np.ascontiguousarray(np.asarray(L0_attn_Wo, dtype=np.float32)),
        "bo2d": np.ascontiguousarray(np.asarray(L0_attn_bo, dtype=np.float32).reshape(8, 128)),
        "gw": np.ascontiguousarray(np.asarray(L0_ffn_gate_w, dtype=np.float32)),
        "wg": np.ascontiguousarray(np.asarray(L0_ffn_w_gate, dtype=np.float32)),
        "wu": np.ascontiguousarray(np.asarray(L0_ffn_w_up, dtype=np.float32)),
        "wd": np.ascontiguousarray(np.asarray(L0_ffn_w_down, dtype=np.float32)),
        "wsg": np.ascontiguousarray(np.asarray(L0_ffn_ws_gate, dtype=np.float32)),
        "wsu": np.ascontiguousarray(np.asarray(L0_ffn_ws_up, dtype=np.float32)),
        "wsd": np.ascontiguousarray(np.asarray(L0_ffn_ws_down, dtype=np.float32)),
    }
    in_maps = []
    for i in range(N_CORES):
        m = dict(common)
        m["x_in"] = np.ascontiguousarray(x[2 * i:2 * i + 2].reshape(T, H))
        in_maps.append(m)

    if _NC_CACHE is None:
        _NC_CACHE = build_program()
    res = run_bass_kernel_spmd(_NC_CACHE, in_maps, core_ids=list(range(N_CORES)))
    out = np.concatenate([r["y_out"].reshape(2, SEQ, H) for r in res.results], axis=0)
    return out


# version 09
# speedup vs baseline: 1.1366x; 1.0473x over previous
import contextlib
import numpy as np
import concourse.bass as bass
import concourse.mybir as mybir
from concourse.bass_utils import run_bass_kernel_spmd

F32 = mybir.dt.float32
BF16 = mybir.dt.bfloat16
AF = mybir.ActivationFunctionType
ALU = mybir.AluOpType
AX = mybir.AxisListType

N_CORES = 8
T = 4096
H = 1024
NE = 8
NPASS = 4
PT = 1024
SEQ = 2048

_DEBUG_XS = False


class Walk:
    ENGS = ("t", "v", "a", "g", "s")

    def __init__(self, nc, tag):
        self.nc = nc
        self.tag = tag
        self.ops = {e: [] for e in self.ENGS}
        self.cnt = {}
        self.sems = {}
        self.waited = {e: {} for e in self.ENGS}
        self.last_rd = {}
        for k in ("mm", "vx", "ac", "sd", "gd"):
            name = tag + k
            self.sems[name] = nc.semaphore(name).__enter__()
            self.cnt[name] = 0
        self.MM, self.VX, self.AC, self.SD, self.GD = (
            tag + "mm", tag + "vx", tag + "ac", tag + "sd", tag + "gd")

    def wait(self, eng, name, val):
        if val is None or val <= 0:
            return
        assert val <= self.cnt[name], f"wait on future {name}:{val} (cnt={self.cnt[name]})"
        if self.waited[eng].get(name, 0) >= val:
            return
        self.waited[eng][name] = val
        s = self.sems[name]
        self.ops[eng].append(lambda e, s=s, v=val: e.wait_ge(s, v))

    def op(self, eng, fn, inc=None, by=1):
        if inc is None:
            self.ops[eng].append(fn)
            return None
        s = self.sems[inc]
        self.cnt[inc] += by
        v = self.cnt[inc]
        self.ops[eng].append(lambda e, fn=fn, s=s, by=by: fn(e).then_inc(s, by))
        return v

    def rd(self, res, kind, val):
        d = self.last_rd.setdefault(res, {})
        if val is not None and val > d.get(kind, 0):
            d[kind] = val

    def guard(self, eng, res):
        for kind, val in self.last_rd.get(res, {}).items():
            self.wait(eng, kind, val)

    def emit(self):
        with self.nc.Block() as blk:
            @blk.tensor
            def _(t):
                for f in self.ops["t"]:
                    f(t)

            @blk.vector
            def _(v):
                for f in self.ops["v"]:
                    f(v)

            @blk.scalar
            def _(a):
                for f in self.ops["a"]:
                    f(a)

            @blk.gpsimd
            def _(g):
                for f in self.ops["g"]:
                    f(g)

            @blk.sync
            def _(s):
                for f in self.ops["s"]:
                    f(s)


def build_program():
    nc = bass.Bass("TRN2", target_bir_lowering=False)

    x_in = nc.dram_tensor("x_in", [T, H], F32, kind="ExternalInput").ap()
    wq_d = nc.dram_tensor("wq", [H, H], F32, kind="ExternalInput").ap()
    wkv_d = nc.dram_tensor("wkv", [H, 2 * H], F32, kind="ExternalInput").ap()
    wo_d = nc.dram_tensor("wo", [H, H], F32, kind="ExternalInput").ap()
    bo_d = nc.dram_tensor("bo2d", [8, 128], F32, kind="ExternalInput").ap()
    gw_d = nc.dram_tensor("gw", [H, NE], F32, kind="ExternalInput").ap()
    wg_d = nc.dram_tensor("wg", [NE, H, 2048], F32, kind="ExternalInput").ap()
    wu_d = nc.dram_tensor("wu", [NE, H, 2048], F32, kind="ExternalInput").ap()
    wd_d = nc.dram_tensor("wd", [NE, 2048, H], F32, kind="ExternalInput").ap()
    wsg_d = nc.dram_tensor("wsg", [H, 2048], F32, kind="ExternalInput").ap()
    wsu_d = nc.dram_tensor("wsu", [H, 2048], F32, kind="ExternalInput").ap()
    wsd_d = nc.dram_tensor("wsd", [2048, H], F32, kind="ExternalInput").ap()
    y_out = nc.dram_tensor("y_out", [T, H], F32, kind="ExternalOutput").ap()
    kind_sc = "ExternalOutput" if _DEBUG_XS else "Internal"
    xs1 = nc.dram_tensor("xs1", [8, 128, T], F32, kind=kind_sc).ap()
    xs2 = nc.dram_tensor("xs2", [8, 128, T], F32, kind=kind_sc).ap()

    ident_d = nc.inline_tensor(np.eye(128, dtype=np.float32), name="ident").ap()
    sel_np = np.zeros((8, 8 * 128), dtype=np.float32)
    for e in range(8):
        sel_np[e, e * 128:(e + 1) * 128] = 1.0
    sel_d = nc.inline_tensor(sel_np, name="sel").ap()
    ones_d = nc.inline_tensor(np.ones((128, 128), dtype=np.float32), name="ones").ap()

    ctx = contextlib.ExitStack()
    ec = ctx.enter_context
    with ctx:
        ident_sb = ec(nc.sbuf_tensor("ident_sb", [128, 128], F32))
        ones_b = ec(nc.sbuf_tensor("ones_b", [128, 128], BF16))
        zeros_b = ec(nc.sbuf_tensor("zeros_b", [128, 16], BF16))
        sel_sb = ec(nc.sbuf_tensor("sel_sb", [8, 1024], F32))
        gw_f = ec(nc.sbuf_tensor("gw_f", [128, 8 * 8], F32))
        sumg_bc = ec(nc.sbuf_tensor("sumg_bc", [128, 8], F32))
        sumg64 = ec(nc.sbuf_tensor("sumg64", [128, 64], F32))
        bo_sb = ec(nc.sbuf_tensor("bo_sb", [128, 8], F32))
        half_c = ec(nc.sbuf_tensor("half_c", [128, 1024], F32))
        cm12 = ec(nc.sbuf_tensor("cm12", [128, 1], F32))
        ceps = ec(nc.sbuf_tensor("ceps", [128, 1], F32))

        with (nc.semaphore("su_d") as su_d, nc.semaphore("su_c") as su_c,
              nc.psum_tensor("su_ps", [128, 512], F32) as su_ps,
              nc.sbuf_tensor("su_tmp", [8, 128], F32) as su_tmp,
              nc.sbuf_tensor("ones_f", [128, 128], F32) as ones_f,
              nc.sbuf_tensor("su_col", [8, 1], F32) as su_col,
              nc.sbuf_tensor("su_row", [1, 8], F32) as su_row,
              nc.Block() as blk):

            @blk.gpsimd
            def _(g):
                g.memset(zeros_b[:, :], 0.0)
                g.memset(half_c[:, :], 0.5)
                g.memset(cm12[:, :], -12.0)
                g.memset(ceps[:, :], 1e-5)
                g.dma_start(ones_b[:, :], ones_d).then_inc(su_d, 16)
                g.dma_start(gw_f.ap().rearrange("p (kk e) -> p kk e", kk=8),
                            gw_d.rearrange("(kk p) e -> p kk e", p=128)).then_inc(su_d, 16)

            @blk.sync
            def _(s):
                s.dma_start(ident_sb[:, :], ident_d).then_inc(su_d, 16)
                s.dma_start(ones_f[:, :], ones_d).then_inc(su_d, 16)
                s.dma_start(sel_sb[:, :], sel_d).then_inc(su_d, 16)
                s.dma_start(su_tmp[:, :], bo_d).then_inc(su_d, 16)

            @blk.tensor
            def _(t):
                t.wait_ge(su_d, 16 * 6)
                t.transpose(su_ps[:, 0:8], su_tmp[:, :], ident_sb[0:8, 0:8])
                for k in range(7):
                    t.matmul(su_ps[0:8, 16:17], gw_f[:, k * 8:(k + 1) * 8], ones_f[:, 0:1],
                             start=(k == 0), stop=False)
                t.matmul(su_ps[0:8, 16:17], gw_f[:, 56:64], ones_f[:, 0:1],
                         start=False, stop=True).then_inc(su_c, 1)

            @blk.vector
            def _(v):
                v.wait_ge(su_c, 1)
                v.tensor_copy(bo_sb[:, :], su_ps[:, 0:8]).then_inc(su_c, 1)
                v.tensor_copy(su_col[:, :], su_ps[0:8, 16:17]).then_inc(su_c, 1)

            @blk.tensor
            def _(t):
                t.wait_ge(su_c, 3)
                t.transpose(su_ps[0:1, 32:40], su_col[:, :], ident_sb[0:8, 0:8]).then_inc(su_c, 1)

            @blk.scalar
            def _(sc):
                sc.wait_ge(su_c, 4)
                sc.copy(su_row[:, :], su_ps[0:1, 32:40]).then_inc(su_c, 1)

            @blk.tensor
            def _(t):
                t.wait_ge(su_c, 5)
                t.matmul(su_ps[:, 48:56], ones_f[0:1, :], su_row[:, :],
                         start=True, stop=True).then_inc(su_c, 1)

            @blk.vector
            def _(v):
                v.wait_ge(su_c, 6)
                v.tensor_copy(sumg_bc[:, :], su_ps[:, 48:56]).then_inc(su_c, 1)
                v.wait_ge(su_c, 7)
                for b8 in range(8):
                    v.tensor_copy(sumg64[:, b8 * 8:b8 * 8 + 8], su_ps[:, 48:56])

        def ffn_phase(tag, src, dst, src_token_major, dst_token_major):
            pctx = contextlib.ExitStack()
            pec = pctx.enter_context
            with pctx:
                accs = [pec(nc.sbuf_tensor(f"{tag}_acc{i}", [128, 8 * PT], F32)) for i in range(2)]
                xln = pec(nc.sbuf_tensor(f"{tag}_xln", [128, 8 * PT], BF16))
                hid = pec(nc.sbuf_tensor(f"{tag}_hid", [128, 16 * PT], BF16))
                wgb = [pec(nc.sbuf_tensor(f"{tag}_wg{i}", [128, 8 * 256], BF16)) for i in range(2)]
                wub = [pec(nc.sbuf_tensor(f"{tag}_wu{i}", [128, 8 * 256], BF16)) for i in range(2)]
                wdb = [pec(nc.sbuf_tensor(f"{tag}_wd{i}", [128, 8 * 256], BF16)) for i in range(4)]
                s_sb = [pec(nc.sbuf_tensor(f"{tag}_s{i}", [128, 512], F32)) for i in range(2)]
                tmp_sb = [pec(nc.sbuf_tensor(f"{tag}_tmp{i}", [128, 512], F32)) for i in range(2)]
                w_bc = pec(nc.sbuf_tensor(f"{tag}_wbc", [128, 1024], F32))
                w_all = pec(nc.sbuf_tensor(f"{tag}_wall", [8, 1024], F32))
                mean_sb = pec(nc.sbuf_tensor(f"{tag}_mean", [128, 1024], F32))
                rstd_sb = pec(nc.sbuf_tensor(f"{tag}_rstd", [128, 1024], F32))
                var_sb = pec(nc.sbuf_tensor(f"{tag}_var", [128, 1024], F32))
                xb_bf = [pec(nc.sbuf_tensor(f"{tag}_xb{i}", [128, 512], BF16)) for i in range(4)]
                x2_bf = [pec(nc.sbuf_tensor(f"{tag}_x2{i}", [128, 512], BF16)) for i in range(4)]
                stage = [pec(nc.sbuf_tensor(f"{tag}_st{i}", [128, 1024], F32)) for i in range(2)]
                r1_sb = pec(nc.sbuf_tensor(f"{tag}_r1", [8, 1024], F32))
                mtok8 = pec(nc.sbuf_tensor(f"{tag}_mtok", [128, 8], F32))
                rtok8 = pec(nc.sbuf_tensor(f"{tag}_rtok", [128, 8], F32))
                rt_all = pec(nc.sbuf_tensor(f"{tag}_rtall", [128, 64], F32))
                rl = [pec(nc.sbuf_tensor(f"{tag}_rl{i}", [128, 64], F32)) for i in range(3)]
                r8 = [pec(nc.sbuf_tensor(f"{tag}_r8{i}", [128, 8], F32)) for i in range(2)]
                wtok_all = pec(nc.sbuf_tensor(f"{tag}_wtall", [128, 64], F32))
                pg = [pec(nc.psum_tensor(f"{tag}_pg{i}", [128, 512], F32)) for i in range(2)]
                pu = [pec(nc.psum_tensor(f"{tag}_pu{i}", [128, 512], F32)) for i in range(2)]
                pd = [pec(nc.psum_tensor(f"{tag}_pd{i}", [128, 512], F32)) for i in range(2)]
                paux = pec(nc.psum_tensor(f"{tag}_paux", [128, 1024], F32))

                w = Walk(nc, tag)
                MM, VX, AC, SD, GD = w.MM, w.VX, w.AC, w.SD, w.GD
                ev = {}

                weight_mats = [(wg_d[e], wu_d[e], wd_d[e]) for e in range(NE)]
                weight_mats.append((wsg_d, wsu_d, wsd_d))

                NQ_TOT = NPASS * (NE + 1) * 8
                NH_TOT = NPASS * (NE + 1) * 4
                ev_qdma = {}
                ev_hdma = {}

                def emit_q(j):
                    if j >= NQ_TOT or j in ev_qdma:
                        return
                    e_ = (j // 8) % (NE + 1)
                    o_ = j % 8
                    bidx = j % 2
                    wg_s, wu_s, _ = weight_mats[e_]
                    w.guard("g", f"wgu{bidx}")
                    for (srcm, dstb) in ((wg_s, wgb[bidx]), (wu_s, wub[bidx])):
                        vd = w.op("g", lambda g, srcm=srcm, dstb=dstb, o_=o_: g.dma_start(
                            dstb.ap().rearrange("x (kk n) -> x kk n", kk=8),
                            srcm.rearrange("(kk x) d -> x kk d", x=128)[:, :, o_ * 256:(o_ + 1) * 256]),
                            inc=GD, by=16)
                    ev_qdma[j] = vd

                def emit_hq(j):
                    if j >= NH_TOT or j in ev_hdma:
                        return
                    e_ = (j // 4) % (NE + 1)
                    hq4 = j % 4
                    bpair = (j % 2) * 2
                    wd_s = weight_mats[e_][2]
                    w.guard("g", f"wd{bpair}")
                    for k2h in range(2):
                        vd = w.op("g", lambda g, wd_s=wd_s, hq4=hq4, k2h=k2h, bpair=bpair: g.dma_start(
                            wdb[bpair + k2h].ap().rearrange("x (kk n) -> x kk n", kk=8),
                            wd_s.rearrange("(kk x) n -> x kk n", x=128)[
                                :, 8 * k2h:8 * (k2h + 1), hq4 * 256:(hq4 + 1) * 256]),
                            inc=GD, by=16)
                    ev_hdma[j] = vd

                for j in range(2):
                    emit_q(j)
                    emit_hq(j)

                for p in range(NPASS):
                    accP = accs[p % 2]
                    accR = f"acc{p % 2}"
                    if src_token_major:
                        for r in range(8):
                            st_buf = stage[r % 2]
                            w.guard("s", f"stage{r % 2}")
                            rows = p * PT + r * 128
                            vd = w.op("s", lambda s, st_buf=st_buf, rows=rows: s.dma_start(
                                st_buf[:, :], src[rows:rows + 128, :]), inc=SD, by=16)
                            w.wait("t", SD, vd)
                            w.guard("t", "paux")
                            for f in range(8):
                                def trf(t, st_buf=st_buf, f=f):
                                    return t.transpose(paux[:, f * 128:(f + 1) * 128],
                                                       st_buf[:, f * 128:(f + 1) * 128], ident_sb[:, :])
                                if f < 7:
                                    w.op("t", trf)
                                else:
                                    etr = w.op("t", trf, inc=MM)
                            w.rd(f"stage{r % 2}", MM, etr)
                            w.wait("v", MM, etr)
                            w.guard("v", accR)
                            acc_v = accP.ap().rearrange("q (kk n) -> q kk n", kk=8)[:, :, r * 128:(r + 1) * 128]
                            pa_v = paux.ap().rearrange("q (kk n) -> q kk n", kk=8)
                            eev = w.op("v", lambda v, acc_v=acc_v, pa_v=pa_v: v.tensor_copy(acc_v, pa_v),
                                       inc=VX)
                            w.rd("paux", VX, eev)
                        ing_done = ("v", VX, eev)
                    else:
                        for k in range(8):
                            w.guard("s", accR)
                            vd = w.op("s", lambda s, k=k, p=p, accP=accP: s.dma_start(
                                accP[:, k * PT:(k + 1) * PT], src[k, :, p * PT:(p + 1) * PT]),
                                inc=SD, by=16)
                        ing_done = ("s", SD, vd)

                    ie0, isem0, ival0 = ing_done
                    for c in range(2):
                        w.wait("t", isem0, ival0)
                        w.guard("t", f"pdbank{c}")
                        for k in range(8):
                            def r1mm(t, k=k, c=c, accP=accP):
                                return t.matmul(pd[c][0:8, 0:512],
                                               gw_f[:, k * 8:(k + 1) * 8],
                                               accP[:, k * PT + c * 512: k * PT + c * 512 + 512],
                                               start=(k == 0), stop=(k == 7))
                            er1 = w.op("t", r1mm, inc=(MM if k == 7 else None))
                        w.rd(accR, MM, er1)
                        w.wait("a", MM, er1)
                        w.guard("a", "r1_sb")
                        er1e = w.op("a", lambda a, c=c: a.copy(
                            r1_sb[:, c * 512:(c + 1) * 512], pd[c][0:8, 0:512]), inc=AC)
                        ev[("r1ev", p, c)] = er1e
                        w.rd(f"pdbank{c}", AC, er1e)

                    ie, isem, ival = ing_done
                    for c in range(2):
                        for k in range(8):
                            i = c * 8 + k
                            bi = i % 4
                            if ie != "v":
                                w.wait("v", isem, ival)
                            w.wait("a", isem, ival)
                            w.guard("v", f"xb{bi}")
                            w.guard("a", f"x2{bi}")
                            asl = accP[:, k * PT + c * 512: k * PT + c * 512 + 512]
                            evb = w.op("v", lambda v, b=xb_bf[bi], asl=asl: v.tensor_copy(b[:, :], asl),
                                       inc=VX)
                            evs = w.op("a", lambda a, b=x2_bf[bi], asl=asl: a.activation(
                                b[:, :], asl, AF.Square), inc=AC)
                            w.rd(accR, VX, evb)
                            w.rd(accR, AC, evs)
                            w.wait("t", VX, evb)
                            w.wait("t", AC, evs)
                            if k == 0:
                                w.guard("t", "paux")
                            w.op("t", lambda t, b=xb_bf[bi], k=k: t.matmul(
                                paux[:, 0:512], ones_b[:, :], b[:, :], start=(k == 0), stop=(k == 7)))
                            def sqm(t, b=x2_bf[bi], k=k):
                                return t.matmul(paux[:, 512:1024], ones_b[:, :], b[:, :],
                                                start=(k == 0), stop=(k == 7))
                            smm = w.op("t", sqm, inc=MM)
                            if k == 0 and c == 0:
                                w.guard("t", "pdbank0")
                                w.op("t", lambda t: t.matmul(
                                    pd[0][:, 0:16], ones_b[:, :], zeros_b[:, :],
                                    start=True, stop=True))
                            for j in range(4):
                                g = c * 4 + j
                                w.op("t", lambda t, b=xb_bf[bi], j=j, g=g, k=k: t.matmul(
                                    pd[0][:, g:g + 1], b[:, j * 128:j * 128 + 128], ones_b[:, 0:1],
                                    start=False, stop=(k == 7)))
                                def tsq(t, b=x2_bf[bi], j=j, g=g, k=k):
                                    return t.matmul(
                                        pd[0][:, 8 + g:9 + g], b[:, j * 128:j * 128 + 128],
                                        ones_b[:, 0:1], start=False, stop=(k == 7))
                                smm = w.op("t", tsq, inc=(MM if j == 3 else None))
                            w.rd(f"xb{bi}", MM, smm)
                            w.rd(f"x2{bi}", MM, smm)
                            if k == 7:
                                ev[("tokstat", p, c)] = smm
                        w.wait("a", MM, smm)
                        w.guard("a", f"mean{c}")
                        ma = w.op("a", lambda a, c=c: a.activation(
                            mean_sb[:, c * 512:(c + 1) * 512], paux[:, 0:512], AF.Copy,
                            scale=1.0 / H), inc=AC)
                        mb = w.op("a", lambda a, c=c: a.activation(
                            var_sb[:, c * 512:(c + 1) * 512], paux[:, 512:1024], AF.Copy,
                            scale=1.0 / H), inc=AC)
                        w.rd("paux", AC, mb)
                        w.wait("v", AC, mb)
                        w.guard("v", f"rstd{c}")
                        e1 = w.op("v", lambda v, c=c: v.tensor_mul(
                            rstd_sb[:, c * 512:(c + 1) * 512], mean_sb[:, c * 512:(c + 1) * 512],
                            mean_sb[:, c * 512:(c + 1) * 512]), inc=VX)
                        w.wait("v", VX, e1)
                        e2 = w.op("v", lambda v, c=c: v.tensor_sub(
                            var_sb[:, c * 512:(c + 1) * 512], var_sb[:, c * 512:(c + 1) * 512],
                            rstd_sb[:, c * 512:(c + 1) * 512]), inc=VX)
                        w.wait("a", VX, e2)
                        e3 = w.op("a", lambda a, c=c: a.activation(
                            rstd_sb[:, c * 512:(c + 1) * 512], var_sb[:, c * 512:(c + 1) * 512],
                            AF.Sqrt, bias=ceps[:, 0:1]), inc=AC)
                        w.wait("v", AC, e3)
                        ev[("rstd", p, c)] = w.op("v", lambda v, c=c: v.reciprocal(
                            rstd_sb[:, c * 512:(c + 1) * 512], rstd_sb[:, c * 512:(c + 1) * 512]),
                            inc=VX)

                    w.wait("a", MM, ev[("tokstat", p, 1)])
                    w.guard("a", "mtok8")
                    tm1 = w.op("a", lambda a: a.activation(
                        mtok8[:, :], pd[0][:, 0:8], AF.Copy, scale=1.0 / H), inc=AC)
                    tm2 = w.op("a", lambda a: a.activation(
                        rtok8[:, :], pd[0][:, 8:16], AF.Copy, scale=1.0 / H), inc=AC)
                    w.rd("pdbank0", AC, tm2)
                    w.wait("v", AC, tm2)
                    w.guard("v", "r8_0")
                    tm3 = w.op("v", lambda v: v.tensor_mul(r8[0][:, :], mtok8[:, :], mtok8[:, :]), inc=VX)
                    w.wait("v", VX, tm3)
                    tm4 = w.op("v", lambda v: v.tensor_sub(rtok8[:, :], rtok8[:, :], r8[0][:, :]), inc=VX)
                    w.wait("a", VX, tm4)
                    tm5 = w.op("a", lambda a: a.activation(
                        rtok8[:, :], rtok8[:, :], AF.Sqrt, bias=ceps[:, 0:1]), inc=AC)
                    w.wait("v", AC, tm5)
                    ev[("rtok", p)] = w.op("v", lambda v: v.reciprocal(rtok8[:, :], rtok8[:, :]), inc=VX)

                    for k in range(8):
                        for c in range(2):
                            w.wait("v", VX, ev[("rstd", p, c)])
                            w.guard("v", "xln")
                            w.guard("v", f"tmp{(k * 2 + c) % 2}")
                            asl = accP[:, k * PT + c * 512: k * PT + c * 512 + 512]
                            msl = mean_sb[:, c * 512:(c + 1) * 512]
                            rsl = rstd_sb[:, c * 512:(c + 1) * 512]
                            tb = tmp_sb[(k * 2 + c) % 2]
                            e1 = w.op("v", lambda v, tb=tb, asl=asl, msl=msl: v.tensor_sub(
                                tb[:, :], asl, msl), inc=VX)
                            w.wait("v", VX, e1)
                            w.rd(accR, VX, e1)
                            xsl = xln[:, k * PT + c * 512: k * PT + c * 512 + 512]
                            e2 = w.op("v", lambda v, xsl=xsl, tb=tb, rsl=rsl: v.tensor_mul(
                                xsl, tb[:, :], rsl), inc=VX)
                            w.rd(f"mean{c}", VX, e1)
                            w.rd(f"rstd{c}", VX, e2)
                            w.rd(f"tmp{(k * 2 + c) % 2}", VX, e2)
                    norm_done = e2

                    w.guard("t", "paux")
                    w.wait("t", AC, ev[("r1ev", p, 1)])
                    w.wait("t", VX, ev[("rstd", p, 1)])
                    for bi_ in range(8):
                        def rtr(t, bi_=bi_):
                            return t.transpose(
                                paux[:, bi_ * 8:bi_ * 8 + 8], r1_sb[:, bi_ * 128:(bi_ + 1) * 128],
                                ident_sb[0:8, 0:8])
                        et3 = w.op("t", rtr, inc=(MM if bi_ == 7 else None))
                    w.rd("r1_sb", MM, et3)
                    def b3(tens, bstride, estride):
                        pap = tens.ap()
                        return bass.AP(tens, 0, [list(pap.ap[0]), [bstride, 8], [estride, 8]])
                    w.wait("v", MM, et3)
                    w.wait("v", VX, ev[("rtok", p)])
                    c0 = w.op("v", lambda v: v.tensor_copy(rt_all[:, :], paux[:, 0:64]), inc=VX)
                    w.rd("paux", VX, c0)
                    w.wait("v", VX, c0)
                    e1 = w.op("v", lambda v: v.tensor_tensor(
                        b3(rl[0], 8, 1), b3(sumg64, 8, 1), b3(mtok8, 1, 0), ALU.mult), inc=VX)
                    w.rd("mtok8", VX, e1)
                    w.wait("v", VX, e1)
                    e2 = w.op("v", lambda v: v.tensor_sub(rl[0][:, :], rt_all[:, :], rl[0][:, :]), inc=VX)
                    w.wait("v", VX, e2)
                    e3 = w.op("v", lambda v: v.tensor_tensor(
                        b3(rl[0], 8, 1), b3(rl[0], 8, 1), b3(rtok8, 1, 0), ALU.mult), inc=VX)
                    w.rd("mtok8", VX, e3)
                    w.rd("r8_0", VX, e3)
                    w.wait("v", VX, e3)
                    e4 = w.op("v", lambda v: v.reduce_max(
                        r8[0][:, :], rl[0].ap().rearrange("p (b e) -> p b e", b=8), AX.X,
                        negate=True), inc=VX)
                    w.wait("v", VX, e4)
                    e5 = w.op("v", lambda v: v.tensor_tensor(
                        b3(rl[2], 8, 1), b3(rl[0], 8, 1), b3(r8[0], 1, 0), ALU.add), inc=VX)
                    w.wait("a", VX, e5)
                    e6 = w.op("a", lambda a: a.activation(rl[1][:, :], rl[2][:, :], AF.Exp), inc=AC)
                    w.wait("v", AC, e6)
                    e7 = w.op("v", lambda v: v.reduce_sum(
                        r8[0][:, :], rl[1].ap().rearrange("p (b e) -> p b e", b=8), AX.X), inc=VX)
                    w.wait("v", VX, e7)
                    e8 = w.op("v", lambda v: v.reciprocal(r8[0][:, :], r8[0][:, :]), inc=VX)
                    w.wait("v", VX, e8)
                    e9 = w.op("v", lambda v: v.tensor_tensor(
                        b3(rl[1], 8, 1), b3(rl[1], 8, 1), b3(r8[0], 1, 0), ALU.mult), inc=VX)
                    w.wait("v", VX, e9)
                    e10 = w.op("v", lambda v: v.reduce_max(
                        r8[0][:, :], rl[1].ap().rearrange("p (b e) -> p b e", b=8), AX.X), inc=VX)
                    w.wait("v", VX, e10)
                    e11 = w.op("v", lambda v: v.tensor_tensor(
                        b3(rl[2], 8, 1), b3(rl[1], 8, 1), b3(r8[0], 1, 0), ALU.is_ge), inc=VX)
                    w.wait("v", VX, e11)
                    e12 = w.op("v", lambda v: v.scalar_tensor_tensor(
                        rl[2][:, :], rl[2][:, :], -1e30, rl[1][:, :], ALU.mult, ALU.add), inc=VX)
                    w.wait("v", VX, e12)
                    e13 = w.op("v", lambda v: v.reduce_max(
                        r8[0][:, :], rl[2].ap().rearrange("p (b e) -> p b e", b=8), AX.X), inc=VX)
                    w.wait("v", VX, e13)
                    e14 = w.op("v", lambda v: v.tensor_tensor(
                        b3(rl[2], 8, 1), b3(rl[1], 8, 1), b3(r8[0], 1, 0), ALU.is_ge), inc=VX)
                    w.wait("v", VX, e14)
                    e15 = w.op("v", lambda v: v.tensor_mul(rl[2][:, :], rl[2][:, :], rl[1][:, :]), inc=VX)
                    w.wait("v", VX, e15)
                    e16 = w.op("v", lambda v: v.reduce_sum(
                        r8[1][:, :], rl[2].ap().rearrange("p (b e) -> p b e", b=8), AX.X), inc=VX)
                    w.wait("v", VX, e16)
                    e17 = w.op("v", lambda v: v.tensor_scalar(
                        r8[1][:, :], r8[1][:, :], 2.0, 2e-20, ALU.mult, ALU.add), inc=VX)
                    w.wait("v", VX, e17)
                    e18 = w.op("v", lambda v: v.reciprocal(r8[1][:, :], r8[1][:, :]), inc=VX)
                    w.wait("v", VX, e18)
                    w.guard("v", "wtok")
                    e19 = w.op("v", lambda v: v.tensor_tensor(
                        b3(wtok_all, 8, 1), b3(rl[2], 8, 1), b3(r8[1], 1, 0), ALU.mult), inc=VX)
                    w.wait("t", VX, e19)
                    for bi_ in range(8):
                        reg = 64 + (bi_ % 2) * 128
                        w.wait("t", AC, ev.get(("wall", p, bi_ - 2), 0))
                        ewt = w.op("t", lambda t, bi_=bi_, reg=reg: t.transpose(
                            paux[0:8, reg:reg + 128], wtok_all[:, bi_ * 8:bi_ * 8 + 8],
                            ident_sb[:, :]), inc=MM)
                        w.wait("a", MM, ewt)
                        w.guard("a", "w_all")
                        ewe = w.op("a", lambda a, bi_=bi_, reg=reg: a.copy(
                            w_all[:, bi_ * 128:(bi_ + 1) * 128], paux[0:8, reg:reg + 128]), inc=AC)
                        ev[("wall", p, bi_)] = ewe
                    w.rd("wtok", MM, ewt)
                    w.rd("paux", AC, ewe)
                    router_done = ev[("wall", p, 7)]

                    for e in range(NE + 1):
                        shared = (e == NE)
                        pe_idx = p * (NE + 1) + e
                        if not shared:
                            w.wait("t", AC, router_done)
                            for c in range(2):
                                w.guard("t", "paux")
                                ewm = w.op("t", lambda t, e=e, c=c: t.matmul(
                                    paux[:, c * 512:(c + 1) * 512],
                                    sel_sb[:, e * 128:(e + 1) * 128],
                                    w_all[:, c * 512:(c + 1) * 512],
                                    start=True, stop=True), inc=MM)
                                w.rd("w_all", MM, ewm)
                                w.wait("a", MM, ewm)
                                w.guard("a", f"wbc{c}")
                                ewc = w.op("a", lambda a, c=c: a.copy(
                                    w_bc[:, c * 512:(c + 1) * 512], paux[:, c * 512:(c + 1) * 512]),
                                    inc=AC)
                                w.rd("paux", AC, ewc)
                                ev[("wbc", p, e, c)] = ewc
                            wbc_t = w_bc
                        else:
                            wbc_t = half_c

                        for d in range(16):
                            oct_ = d // 2
                            jq = pe_idx * 8 + oct_
                            bidx = jq % 2
                            do = d % 2
                            for c in range(2):
                                w.wait("t", GD, ev_qdma[jq])
                                w.wait("t", VX, norm_done)
                                w.guard("t", f"pgbank{c}")
                                for k in range(8):
                                    def gmm(t, bidx=bidx, do=do, k=k, c=c):
                                        return t.matmul(
                                            pg[c][:, :],
                                            wgb[bidx][:, k * 256 + do * 128: k * 256 + do * 128 + 128],
                                            xln[:, k * PT + c * 512: k * PT + c * 512 + 512],
                                            start=(k == 0), stop=(k == 7))
                                    egm = w.op("t", gmm, inc=(MM if k == 7 else None))
                                w.guard("t", f"pubank{c}")
                                for k in range(8):
                                    def umm(t, bidx=bidx, do=do, k=k, c=c):
                                        return t.matmul(
                                            pu[c][:, :],
                                            wub[bidx][:, k * 256 + do * 128: k * 256 + do * 128 + 128],
                                            xln[:, k * PT + c * 512: k * PT + c * 512 + 512],
                                            start=(k == 0), stop=(k == 7))
                                    eum = w.op("t", umm, inc=(MM if k == 7 else None))
                                w.rd("xln", MM, eum)
                                w.rd(f"wgu{bidx}", MM, eum)
                                w.wait("a", MM, egm)
                                w.guard("a", f"s{c}")
                                esl = w.op("a", lambda a, c=c: a.activation(
                                    s_sb[c][:, :], pg[c][:, :], AF.Silu), inc=AC)
                                w.rd(f"pgbank{c}", AC, esl)
                                w.wait("v", MM, eum)
                                w.wait("v", AC, esl)
                                w.guard("v", f"hid{c}")
                                hsl = hid[:, d * PT + c * 512: d * PT + c * 512 + 512]
                                eml = w.op("v", lambda v, hsl=hsl, c=c: v.tensor_mul(
                                    hsl, s_sb[c][:, :], pu[c][:, :]), inc=VX)
                                w.rd(f"pubank{c}", VX, eml)
                                w.rd(f"s{c}", VX, eml)
                                ev[("mult", p, e, d, c)] = eml
                            if d % 2 == 1:
                                emit_q(pe_idx * 8 + d // 2 + 2)

                        for h in range(8):
                            hq4 = h // 2
                            jh = pe_idx * 4 + hq4
                            bpair = (jh % 2) * 2
                            ho = h % 2
                            for c in range(2):
                                w.wait("t", GD, ev_hdma[jh])
                                w.wait("t", VX, ev[("mult", p, e, 15, c)])
                                w.guard("t", f"pdbank{c}")
                                for k2 in range(16):
                                    def dmm(t, bpair=bpair, k2=k2, ho=ho, c=c):
                                        return t.matmul(
                                            pd[c][:, :],
                                            wdb[bpair + k2 // 8][:, (k2 % 8) * 256 + ho * 128:
                                                                 (k2 % 8) * 256 + ho * 128 + 128],
                                            hid[:, k2 * PT + c * 512: k2 * PT + c * 512 + 512],
                                            start=(k2 == 0), stop=(k2 == 15))
                                    edm = w.op("t", dmm, inc=(MM if k2 == 15 else None))
                                w.rd(f"hid{c}", MM, edm)
                                w.rd(f"wd{bpair}", MM, edm)
                                w.wait("v", MM, edm)
                                if not shared:
                                    w.wait("v", AC, ev[("wbc", p, e, c)])
                                tb = tmp_sb[c]
                                w.guard("v", f"tmp{c}")
                                etm = w.op("v", lambda v, tb=tb, c=c, wbc_t=wbc_t: v.tensor_mul(
                                    tb[:, :], wbc_t[:, c * 512:(c + 1) * 512], pd[c][:, :]), inc=VX)
                                w.rd(f"pdbank{c}", VX, etm)
                                if not shared:
                                    w.rd(f"wbc{c}", VX, etm)
                                w.wait("v", VX, etm)
                                asl = accP[:, h * PT + c * 512: h * PT + c * 512 + 512]
                                ead = w.op("v", lambda v, asl=asl, tb=tb: v.tensor_add(
                                    asl, asl, tb[:, :]), inc=VX)
                                w.rd(f"tmp{c}", VX, ead)
                                ev[("accadd", p, e, h, c)] = ead
                            if h % 2 == 1:
                                emit_hq(pe_idx * 4 + h // 2 + 2)
                    if dst_token_major:
                        for r in range(8):
                            c = r // 4
                            ob = stage[r % 2]
                            w.wait("t", VX, ev[("accadd", p, NE, 7, c)])
                            w.guard("t", "paux")
                            for f in range(8):
                                def etr2(t, f=f, r=r, accP=accP):
                                    return t.transpose(paux[:, f * 128:(f + 1) * 128],
                                                       accP[:, f * PT + r * 128: f * PT + r * 128 + 128],
                                                       ident_sb[:, :])
                                etr = w.op("t", etr2, inc=(MM if f == 7 else None))
                            w.rd(accR, MM, etr)
                            w.wait("v", MM, etr)
                            w.guard("v", f"stage{r % 2}")
                            eev = w.op("v", lambda v, ob=ob: v.tensor_copy(ob[:, :], paux[:, :]), inc=VX)
                            w.rd("paux", VX, eev)
                            w.wait("s", VX, eev)
                            rows = p * PT + r * 128
                            edr = w.op("s", lambda s, ob=ob, rows=rows: s.dma_start(
                                dst[rows:rows + 128, :], ob[:, :]), inc=SD, by=16)
                            w.rd(f"stage{r % 2}", SD, edr)
                    else:
                        for k in range(8):
                            w.wait("s", VX, ev[("accadd", p, NE, k, 1)])
                            edr = w.op("s", lambda s, k=k, p=p, accP=accP: s.dma_start(
                                dst[k, :, p * PT:(p + 1) * PT], accP[:, k * PT:(k + 1) * PT]),
                                inc=SD, by=16)
                            w.rd(accR, SD, edr)

                w.wait("s", SD, w.cnt[SD])
                w.wait("g", GD, w.cnt[GD])
                w.emit()

        def attn_phase(tag, src, dst):
            pctx = contextlib.ExitStack()
            pec = pctx.enter_context
            with pctx:
                xln2 = pec(nc.sbuf_tensor(f"{tag}_xln", [128, 8 * SEQ], BF16))
                attn = xln2
                q_bf = pec(nc.sbuf_tensor(f"{tag}_q", [128, 8 * SEQ], BF16))
                k_bf = pec(nc.sbuf_tensor(f"{tag}_k", [128, 8 * SEQ], BF16))
                vT = pec(nc.sbuf_tensor(f"{tag}_vT", [128, 16 * H], BF16))
                wbuf = [pec(nc.sbuf_tensor(f"{tag}_wb{i}", [128, 8 * 1024], BF16)) for i in range(2)]
                xc = [pec(nc.sbuf_tensor(f"{tag}_xc{i}", [128, 512], F32)) for i in range(2)]
                xbb = [pec(nc.sbuf_tensor(f"{tag}_xbb{i}", [128, 512], BF16)) for i in range(2)]
                x2b = [pec(nc.sbuf_tensor(f"{tag}_x2b{i}", [128, 512], BF16)) for i in range(2)]
                mean_sb = pec(nc.sbuf_tensor(f"{tag}_mean", [128, 2048], F32))
                rstd_sb = pec(nc.sbuf_tensor(f"{tag}_rstd", [128, 2048], F32))
                var_sb = pec(nc.sbuf_tensor(f"{tag}_var", [128, 512], F32))
                probs = [pec(nc.sbuf_tensor(f"{tag}_pr{i}", [128, 1024], BF16)) for i in range(2)]
                inv_sb = [pec(nc.sbuf_tensor(f"{tag}_inv0", [128, 512], F32))] * 2
                t_sb = [pec(nc.sbuf_tensor(f"{tag}_t{i}", [128, 512], F32)) for i in range(2)]
                o_sb = [pec(nc.sbuf_tensor(f"{tag}_o0", [128, 512], F32))] * 2
                psS = [pec(nc.psum_tensor(f"{tag}_pS{i}", [128, 1024], F32)) for i in range(2)]
                pav = pec(nc.psum_tensor(f"{tag}_pav", [128, 512], F32))
                psm = pec(nc.psum_tensor(f"{tag}_psm", [128, 512], F32))
                po = [pec(nc.psum_tensor(f"{tag}_po{i}", [128, 512], F32)) for i in range(2)]

                w = Walk(nc, tag)
                MM, VX, AC, SD, GD = w.MM, w.VX, w.AC, w.SD, w.GD
                ev = {}

                for b in range(2):
                    base = b * SEQ
                    vTf = vT.ap().bitcast(F32)
                    sumreg = [psS[0][:, 0:512], psS[0][:, 512:1024],
                              psS[1][:, 0:512], psS[1][:, 512:1024]]
                    sqreg = [pav[:, :], psm[:, :], po[0][:, :], po[1][:, :]]
                    sum_res = ["psS0", "psS0", "psS1", "psS1"]
                    sq_res = ["pav", "psm", "po0", "po1"]
                    for k in range(8):
                        slot = vTf[:, (k % 2) * 2048:(k % 2) * 2048 + 2048]
                        w.guard("s", "vT")
                        vd = w.op("s", lambda s, slot=slot, k=k, base=base: s.dma_start(
                            slot, src[k, :, base: base + 2048]), inc=SD, by=16)
                        for c in range(4):
                            i = k * 4 + c
                            j = i % 2
                            w.wait("v", SD, vd)
                            w.wait("a", SD, vd)
                            w.guard("v", f"xbb{j}")
                            w.guard("a", f"x2b{j}")
                            ssl = slot[:, c * 512:(c + 1) * 512]
                            evb = w.op("v", lambda v, ssl=ssl, j=j: v.tensor_copy(
                                xbb[j][:, :], ssl), inc=VX)
                            evs = w.op("a", lambda a, ssl=ssl, j=j: a.activation(
                                x2b[j][:, :], ssl, AF.Square), inc=AC)
                            w.rd("vT", VX, evb)
                            w.rd("vT", AC, evs)
                            w.wait("t", VX, evb)
                            w.wait("t", AC, evs)
                            if k == 0:
                                w.guard("t", sum_res[c])
                                w.guard("t", sq_res[c])
                            w.op("t", lambda t, j=j, k=k, c=c: t.matmul(
                                sumreg[c], ones_b[:, :], xbb[j][:, :],
                                start=(k == 0), stop=(k == 7)))
                            smm = w.op("t", lambda t, j=j, k=k, c=c: t.matmul(
                                sqreg[c], ones_b[:, :], x2b[j][:, :],
                                start=(k == 0), stop=(k == 7)), inc=MM)
                            w.rd(f"xbb{j}", MM, smm)
                            w.rd(f"x2b{j}", MM, smm)
                            if k == 7:
                                ev[("bstat", b, c)] = smm
                    for c in range(4):
                        cc = 0
                        w.wait("a", MM, ev[("bstat", b, c)])
                        w.guard("a", f"mean{c}")
                        ma = w.op("a", lambda a, c=c: a.activation(
                            mean_sb[:, c * 512:(c + 1) * 512], sumreg[c], AF.Copy, scale=1.0 / H),
                            inc=AC)
                        w.guard("a", f"var{cc}")
                        mb = w.op("a", lambda a, c=c, cc=cc: a.activation(
                            var_sb[:, cc * 512:(cc + 1) * 512], sqreg[c], AF.Copy, scale=1.0 / H),
                            inc=AC)
                        w.rd(sum_res[c], AC, ma)
                        w.rd(sq_res[c], AC, mb)
                        w.wait("v", AC, mb)
                        w.guard("v", f"rstd{c}")
                        e1 = w.op("v", lambda v, c=c, cc=cc: v.tensor_mul(
                            rstd_sb[:, c * 512:(c + 1) * 512], mean_sb[:, c * 512:(c + 1) * 512],
                            mean_sb[:, c * 512:(c + 1) * 512]), inc=VX)
                        w.wait("v", VX, e1)
                        e2 = w.op("v", lambda v, c=c, cc=cc: v.tensor_sub(
                            var_sb[:, cc * 512:(cc + 1) * 512], var_sb[:, cc * 512:(cc + 1) * 512],
                            rstd_sb[:, c * 512:(c + 1) * 512]), inc=VX)
                        w.rd(f"mean{c}", VX, e1)
                        w.wait("a", VX, e2)
                        e3 = w.op("a", lambda a, c=c, cc=cc: a.activation(
                            rstd_sb[:, c * 512:(c + 1) * 512], var_sb[:, cc * 512:(cc + 1) * 512],
                            AF.Sqrt, bias=ceps[:, 0:1]), inc=AC)
                        w.rd(f"var{cc}", AC, e3)
                        w.wait("v", AC, e3)
                        ev[("rstdB", b, c)] = w.op("v", lambda v, c=c: v.reciprocal(
                            rstd_sb[:, c * 512:(c + 1) * 512], rstd_sb[:, c * 512:(c + 1) * 512]),
                            inc=VX)
                    for k in range(8):
                        slot = vTf[:, (k % 2) * 2048:(k % 2) * 2048 + 2048]
                        w.guard("s", "vT")
                        vd = w.op("s", lambda s, slot=slot, k=k, base=base: s.dma_start(
                            slot, src[k, :, base: base + 2048]), inc=SD, by=16)
                        for c in range(4):
                            w.wait("v", SD, vd)
                            w.wait("v", VX, ev[("rstdB", b, c)])
                            jj = (k * 4 + c) % 2
                            w.guard("v", f"t{jj}")
                            w.guard("v", "xln2")
                            tb = t_sb[jj]
                            ssl = slot[:, c * 512:(c + 1) * 512]
                            e4 = w.op("v", lambda v, tb=tb, ssl=ssl, c=c: v.tensor_sub(
                                tb[:, :], ssl, mean_sb[:, c * 512:(c + 1) * 512]), inc=VX)
                            w.rd("vT", VX, e4)
                            w.rd(f"mean{c}", VX, e4)
                            w.wait("v", VX, e4)
                            xsl = xln2[:, k * SEQ + c * 512: k * SEQ + c * 512 + 512]
                            e5 = w.op("v", lambda v, xsl=xsl, tb=tb, c=c: v.tensor_mul(
                                xsl, tb[:, :], rstd_sb[:, c * 512:(c + 1) * 512]), inc=VX)
                            w.rd(f"t{jj}", VX, e5)
                            w.rd(f"rstd{c}", VX, e5)
                    ln_done = e5

                    w.guard("g", "wbuf0")
                    ev_wq = w.op("g", lambda g: g.dma_start(
                        wbuf[0].ap().rearrange("x (kk n) -> x kk n", kk=8),
                        wq_d.rearrange("(kk x) n -> x kk n", x=128)), inc=GD, by=16)
                    w.guard("g", "wbuf1")
                    ev_wk = w.op("g", lambda g: g.dma_start(
                        wbuf[1].ap().rearrange("x (kk n) -> x kk n", kk=8),
                        wkv_d.rearrange("(kk x) n -> x kk n", x=128)[:, :, 0:1024]), inc=GD, by=16)

                    def proj(wb_i, out_sb, out_res, wdma_ev, evac_eng, res_tag):
                        last = None
                        for m in range(8):
                            for c in range(4):
                                i = m * 4 + c
                                pb = po[i % 2]
                                w.wait("t", GD, wdma_ev)
                                w.wait("t", VX, ln_done)
                                w.guard("t", f"po{i % 2}")
                                for k in range(8):
                                    def pmm(t, pb=pb, m=m, c=c, k=k, wb_i=wb_i):
                                        return t.matmul(
                                            pb[:, :],
                                            wbuf[wb_i][:, k * 1024 + m * 128: k * 1024 + m * 128 + 128],
                                            xln2[:, k * SEQ + c * 512: k * SEQ + c * 512 + 512],
                                            start=(k == 0), stop=(k == 7))
                                    emm = w.op("t", pmm, inc=(MM if k == 7 else None))
                                w.rd(f"wbuf{wb_i}", MM, emm)
                                w.rd("xln2", MM, emm)
                                w.wait(evac_eng, MM, emm)
                                osl = out_sb[:, m * SEQ + c * 512: m * SEQ + c * 512 + 512]
                                if evac_eng == "v":
                                    eev = w.op("v", lambda v, osl=osl, pb=pb: v.tensor_copy(osl, pb[:, :]),
                                               inc=VX)
                                    w.rd(f"po{i % 2}", VX, eev)
                                else:
                                    eev = w.op("a", lambda a, osl=osl, pb=pb: a.copy(osl, pb[:, :]),
                                               inc=AC)
                                    w.rd(f"po{i % 2}", AC, eev)
                                ev[(res_tag, b, m, c)] = eev
                                last = emm
                        return last

                    qproj_mm = proj(0, q_bf, None, ev_wq, "v", "qev")
                    kproj_mm = proj(1, k_bf, None, ev_wk, "v", "kev")

                    w.guard("g", "wbuf0")
                    ev_wv = w.op("g", lambda g: g.dma_start(
                        wbuf[0].ap().rearrange("x (kk n) -> x kk n", kk=8),
                        wkv_d.rearrange("(kk x) n -> x kk n", x=128)[:, :, 1024:2048]), inc=GD, by=16)

                    for kc in range(16):
                        for n in range(2):
                            i = kc * 2 + n
                            pb = po[i % 2]
                            w.wait("t", GD, ev_wv)
                            w.guard("t", f"po{i % 2}")
                            for k in range(8):
                                def vmm(t, pb=pb, kc=kc, n=n, k=k):
                                    return t.matmul(
                                        pb[:, :],
                                        xln2[:, k * SEQ + kc * 128: k * SEQ + kc * 128 + 128],
                                        wbuf[0][:, k * 1024 + n * 512: k * 1024 + n * 512 + 512],
                                        start=(k == 0), stop=(k == 7))
                                emm = w.op("t", vmm, inc=(MM if k == 7 else None))
                            w.rd("wbuf0", MM, emm)
                            w.rd("xln2", MM, emm)
                            w.wait("a", MM, emm)
                            w.guard("a", "vT")
                            vsl = vT[:, kc * H + n * 512: kc * H + n * 512 + 512]
                            eev = w.op("a", lambda a, vsl=vsl, pb=pb: a.copy(vsl, pb[:, :]), inc=AC)
                            w.rd(f"po{i % 2}", AC, eev)
                    vT_done = eev

                    w.guard("g", "wbuf1")
                    ev_wo = w.op("g", lambda g: g.dma_start(
                        wbuf[1].ap().rearrange("x (kk n) -> x kk n", kk=8),
                        wo_d.rearrange("(kk x) n -> x kk n", x=128)), inc=GD, by=16)

                    for hp in range(8):
                        hA = 2 * hp
                        for qg in range(4):
                            gidx = hp * 4 + qg
                            pavT = [pav, po[0]][gidx % 2]
                            psmT = [psm, po[1]][gidx % 2]
                            pavR = ["pav", "po0"][gidx % 2]
                            psmR = ["psm", "po1"][gidx % 2]
                            w.guard("t", pavR)
                            w.guard("t", psmR)

                            def emit_scores(kc, hp=hp, qg=qg):
                                j = kc % 2
                                sS = psS[j]
                                w.wait("t", VX, ev[("kev", b, hp, 3)])
                                w.wait("t", VX, ev[("qev", b, hp, 3)])
                                w.guard("t", f"psS{j}")
                                def smmA(t, sS=sS, kc=kc, hp=hp, qg=qg):
                                    return t.matmul(
                                        sS[:, 0:512],
                                        k_bf[0:64, hp * SEQ + kc * 128: hp * SEQ + kc * 128 + 128],
                                        q_bf[0:64, hp * SEQ + qg * 512: hp * SEQ + qg * 512 + 512],
                                        start=True, stop=True, tile_position=(0, 0))
                                w.op("t", smmA)
                                def smmB(t, sS=sS, kc=kc, hp=hp, qg=qg):
                                    return t.matmul(
                                        sS[:, 512:1024],
                                        k_bf[64:128, hp * SEQ + kc * 128: hp * SEQ + kc * 128 + 128],
                                        q_bf[64:128, hp * SEQ + qg * 512: hp * SEQ + qg * 512 + 512],
                                        start=True, stop=True, tile_position=(64, 0))
                                return w.op("t", smmB, inc=MM)

                            ev_s = {0: emit_scores(0)}
                            for kc in range(16):
                                j = kc % 2
                                if kc + 1 < 16:
                                    ev_s[kc + 1] = emit_scores(kc + 1)
                                pP = probs[j]
                                w.wait("a", MM, ev_s[kc])
                                w.guard("a", f"pr{j}")
                                eex = w.op("a", lambda a, pP=pP, sS=psS[j]: a.activation(
                                    pP[:, :], sS[:, :], AF.Exp, bias=cm12[:, 0:1], scale=0.125),
                                    inc=AC)
                                w.rd(f"psS{j}", AC, eex)
                                w.wait("t", AC, eex)
                                if kc == 0:
                                    w.wait("t", AC, vT_done)
                                def avA(t, pP=pP, hA=hA, kc=kc, pavT=pavT):
                                    return t.matmul(
                                        pavT[0:64, :],
                                        vT[:, kc * H + hA * 64: kc * H + hA * 64 + 64],
                                        pP[:, 0:512], start=(kc == 0), stop=(kc == 15),
                                        tile_position=(0, 0))
                                def avB(t, pP=pP, hA=hA, kc=kc, pavT=pavT):
                                    return t.matmul(
                                        pavT[64:128, :],
                                        vT[:, kc * H + hA * 64 + 64: kc * H + hA * 64 + 128],
                                        pP[:, 512:1024], start=(kc == 0), stop=(kc == 15),
                                        tile_position=(0, 64))
                                def smA(t, pP=pP, kc=kc, psmT=psmT):
                                    return t.matmul(
                                        psmT[0:64, :], ones_b[:, 0:64], pP[:, 0:512],
                                        start=(kc == 0), stop=(kc == 15), tile_position=(0, 0))
                                def smB(t, pP=pP, kc=kc, psmT=psmT):
                                    return t.matmul(
                                        psmT[64:128, :], ones_b[:, 0:64], pP[:, 512:1024],
                                        start=(kc == 0), stop=(kc == 15), tile_position=(0, 64))
                                w.op("t", avA)
                                w.op("t", avB)
                                w.op("t", smA)
                                eprd = w.op("t", smB, inc=MM)
                                w.rd(f"pr{j}", MM, eprd)
                                w.rd("vT", MM, eprd)
                            w.wait("v", MM, eprd)
                            iv = inv_sb[0]
                            w.guard("v", "inv0")
                            er = w.op("v", lambda v, iv=iv, psmT=psmT: v.reciprocal(iv[:, :], psmT[:, :]), inc=VX)
                            w.rd(psmR, VX, er)
                            w.wait("v", VX, er)
                            w.guard("v", "xln2")
                            asl = attn[:, hp * SEQ + qg * 512: hp * SEQ + qg * 512 + 512]
                            eat = w.op("v", lambda v, asl=asl, iv=iv, pavT=pavT: v.tensor_mul(
                                asl, pavT[:, :], iv[:, :]), inc=VX)
                            w.rd(pavR, VX, eat)
                            w.rd("inv0", VX, eat)
                            ev[("atev", b, hp, qg)] = eat

                    for m in range(8):
                        for c in range(4):
                            i = m * 4 + c
                            pb = po[i % 2]
                            w.wait("t", GD, ev_wo)
                            for hp in range(8):
                                w.wait("t", VX, ev[("atev", b, hp, c)])
                            w.guard("t", f"po{i % 2}")
                            for k in range(8):
                                def omm(t, pb=pb, m=m, c=c, k=k):
                                    return t.matmul(
                                        pb[:, :],
                                        wbuf[1][:, k * 1024 + m * 128: k * 1024 + m * 128 + 128],
                                        attn[:, k * SEQ + c * 512: k * SEQ + c * 512 + 512],
                                        start=(k == 0), stop=(k == 7))
                                emm = w.op("t", omm, inc=(MM if k == 7 else None))
                            w.rd("wbuf1", MM, emm)
                            w.rd("xln2", MM, emm)
                            ob = o_sb[0]
                            xb = xc[i % 2]
                            w.guard("s", f"xc{i % 2}")
                            vd = w.op("s", lambda s, xb=xb, m=m, c=c, base=base: s.dma_start(
                                xb[:, :], src[m, :, base + c * 512: base + c * 512 + 512]),
                                inc=SD, by=16)
                            w.wait("v", SD, vd)
                            w.wait("v", MM, emm)
                            w.guard("v", "o0")
                            ead = w.op("v", lambda v, ob=ob, pb=pb, xb=xb, m=m: v.scalar_tensor_tensor(
                                ob[:, :], pb[:, :], bo_sb[:, m:m + 1], xb[:, :], ALU.add, ALU.add),
                                inc=VX)
                            w.rd(f"po{i % 2}", VX, ead)
                            w.rd(f"xc{i % 2}", VX, ead)
                            w.wait("s", VX, ead)
                            eds = w.op("s", lambda s, ob=ob, m=m, c=c, base=base: s.dma_start(
                                dst[m, :, base + c * 512: base + c * 512 + 512], ob[:, :]),
                                inc=SD, by=16)
                            w.rd("o0", SD, eds)

                w.wait("s", SD, w.cnt[SD])
                w.wait("g", GD, w.cnt[GD])
                w.emit()

        ffn_phase("A", x_in, xs1, True, False)
        attn_phase("B", xs1, xs2)
        ffn_phase("C", xs2, y_out, False, True)

    return nc


_NC_CACHE = None


def kernel(L0_attn_hidden_states, L0_attn_Wq, L0_attn_Wkv, L0_attn_Wo, L0_attn_bo,
           L0_ffn_gate_w, L0_ffn_w_gate, L0_ffn_w_up, L0_ffn_w_down,
           L0_ffn_ws_gate, L0_ffn_ws_up, L0_ffn_ws_down):
    global _NC_CACHE
    x = np.ascontiguousarray(np.asarray(L0_attn_hidden_states, dtype=np.float32))
    common = {
        "wq": np.ascontiguousarray(np.asarray(L0_attn_Wq, dtype=np.float32)),
        "wkv": np.ascontiguousarray(np.asarray(L0_attn_Wkv, dtype=np.float32)),
        "wo": np.ascontiguousarray(np.asarray(L0_attn_Wo, dtype=np.float32)),
        "bo2d": np.ascontiguousarray(np.asarray(L0_attn_bo, dtype=np.float32).reshape(8, 128)),
        "gw": np.ascontiguousarray(np.asarray(L0_ffn_gate_w, dtype=np.float32)),
        "wg": np.ascontiguousarray(np.asarray(L0_ffn_w_gate, dtype=np.float32)),
        "wu": np.ascontiguousarray(np.asarray(L0_ffn_w_up, dtype=np.float32)),
        "wd": np.ascontiguousarray(np.asarray(L0_ffn_w_down, dtype=np.float32)),
        "wsg": np.ascontiguousarray(np.asarray(L0_ffn_ws_gate, dtype=np.float32)),
        "wsu": np.ascontiguousarray(np.asarray(L0_ffn_ws_up, dtype=np.float32)),
        "wsd": np.ascontiguousarray(np.asarray(L0_ffn_ws_down, dtype=np.float32)),
    }
    in_maps = []
    for i in range(N_CORES):
        m = dict(common)
        m["x_in"] = np.ascontiguousarray(x[2 * i:2 * i + 2].reshape(T, H))
        in_maps.append(m)

    if _NC_CACHE is None:
        _NC_CACHE = build_program()
    res = run_bass_kernel_spmd(_NC_CACHE, in_maps, core_ids=list(range(N_CORES)))
    out = np.concatenate([r["y_out"].reshape(2, SEQ, H) for r in res.results], axis=0)
    return out
